# version 11
# baseline (speedup 1.0000x reference)
import numpy as np
import concourse.bass as bass
import concourse.mybir as mybir
from concourse.bass_utils import run_bass_kernel_spmd
from contextlib import ExitStack

F32 = mybir.dt.float32
BF16 = mybir.dt.bfloat16
AF = mybir.ActivationFunctionType
ALU = mybir.AluOpType
AX = mybir.AxisListType

D = 1024
TPC = 2048
NS = 4
SL = 512
NI = 16
NEXP = 9
ALPHA = 1.702
LIMIT = 7.0
C_EFF = 0.5 / ALPHA
C_SH = 0.5 / ALPHA
C_B2 = 2.0 * ALPHA * ALPHA
EPS = 1e-5

ENG = ("pe", "act", "dve", "pool", "sp")



def _perm_gl():
    return np.concatenate([np.arange(0, 4096, 2), np.arange(1, 4096, 2)])


def _chunk_w1(w):
    return np.ascontiguousarray(
        w.reshape(8, 128, 32, 128).transpose(2, 1, 0, 3).reshape(32, 128, 1024))


def _chunk_w2(w):
    return np.ascontiguousarray(
        w.reshape(16, 128, 8, 128).transpose(2, 1, 0, 3).reshape(8, 128, 2048))


def _stage1_stream(w1c, w3c):
    out = np.empty((64, 128, 1024), dtype=np.float32)
    for i in range(16):
        out[4 * i + 0] = w1c[i]
        out[4 * i + 1] = w3c[i]
        out[4 * i + 2] = w1c[i + 16]
        out[4 * i + 3] = w3c[i + 16]
    return out


def prep_host(inputs):
    import ml_dtypes
    g = {k: np.asarray(v) for k, v in inputs.items()}
    perm = _perm_gl()

    x = np.ascontiguousarray(
        g["L0_attn_x"].astype(np.float32, copy=False).reshape(16384, 1024))
    shards = [np.ascontiguousarray(x[c * TPC:(c + 1) * TPC]) for c in range(8)]

    ws1 = np.empty((NEXP, 64, 128, 1024), dtype=np.float32)
    ws2 = np.empty((NEXP, 8, 128, 2048), dtype=np.float32)
    b1h = np.empty((128, NEXP * 32), dtype=np.float32)
    b3h = np.empty((128, NEXP * 32), dtype=np.float32)
    for e in range(8):
        ws1[e] = _stage1_stream(_chunk_w1(g["L0_ffn_w1"][e][:, perm]),
                                _chunk_w1(g["L0_ffn_w3"][e][:, perm]))
        ws2[e] = _chunk_w2(g["L0_ffn_w2"][e])
        b1h[:, e * 32:(e + 1) * 32] = g["L0_ffn_b1"][e][perm].reshape(32, 128).T
        b3h[:, e * 32:(e + 1) * 32] = g["L0_ffn_b3"][e][perm].reshape(32, 128).T
    ws1[8] = _stage1_stream(_chunk_w1(g["L0_ffn_sw1"][:, perm]),
                            _chunk_w1(g["L0_ffn_sw3"][:, perm]))
    ws2[8] = _chunk_w2(g["L0_ffn_sw2"])
    b1h[:, 8 * 32:9 * 32] = g["L0_ffn_sb1"][perm].reshape(32, 128).T
    b3h[:, 8 * 32:9 * 32] = g["L0_ffn_sb3"][perm].reshape(32, 128).T

    gwh = np.ascontiguousarray(
        g["L0_ffn_gate_w"].reshape(8, 8, 128).transpose(2, 1, 0).reshape(128, 64))
    gbh = np.ascontiguousarray(g["L0_ffn_gate_b"].reshape(1, 8).astype(np.float32))

    pjh = np.ascontiguousarray(
        g["L0_attn_proj_matrix"].reshape(8, 128, 8, 128).transpose(3, 0, 2, 1).reshape(128, 8192))
    iwh = np.ascontiguousarray(
        g["L0_attn_input_weights"].reshape(1024, 8, 128).transpose(2, 1, 0).reshape(128, 8192))
    xisfm = np.ascontiguousarray(g["L0_attn_xis"].reshape(8, 128).T)
    xisrow = np.ascontiguousarray(g["L0_attn_xis"].reshape(1, 1024))

    b2 = np.ascontiguousarray(g["L0_ffn_b2"].astype(np.float32))
    sb2fm = np.ascontiguousarray(g["L0_ffn_sb2"].reshape(8, 128).T)

    ident = np.eye(128, dtype=np.float32)
    onesb = np.ones((128, 128), dtype=ml_dtypes.bfloat16)
    selb = np.zeros((8, 1024), dtype=np.float32)
    for e in range(8):
        selb[e, e * 128:(e + 1) * 128] = 1.0
    selb = selb.astype(ml_dtypes.bfloat16)

    common = dict(ws1=ws1, ws2=ws2, b1h=b1h, b3h=b3h, gwh=gwh, gbh=gbh,
                  pjh=pjh, iwh=iwh, xisfm=xisfm, xisrow=xisrow, b2=b2,
                  sb2fm=sb2fm, ident=ident, onesb=onesb, selb=selb)
    return common, shards



class Buf:
    __slots__ = ("name", "rng", "space", "w", "r")

    def __init__(self, name, space=None, rng=None):
        self.name = name
        self.space = space
        self.rng = rng
        self.w = None
        self.r = {}


class Prog:
    def __init__(self, nc, ctx):
        self.nc = nc
        self.ctx = ctx
        self.ops = {e: [] for e in ENG}
        self.sem = {}
        self.cnt = {}
        self.known = {e: {} for e in ENG}
        self.bufs = []
        for name in ("pe", "act", "dve", "pool", "sp"):
            self.sem[name] = ctx.enter_context(nc.semaphore("sem_" + name))
            self.cnt[name] = 0

    def dsem(self, name):
        name = "d_" + name
        if name not in self.sem:
            self.sem[name] = self.ctx.enter_context(self.nc.semaphore(name))
            self.cnt[name] = 0
        return name

    def buf(self, name, space=None, rng=None):
        b = Buf(name, space, rng)
        self.bufs.append(b)
        return b

    def _overlaps(self, b):
        if b.rng is None:
            return [b]
        out = []
        for o in self.bufs:
            if o.space == b.space and o.rng is not None and \
                    o.rng[0] < b.rng[1] and b.rng[0] < o.rng[1]:
                out.append(o)
        return out

    def op(self, eng, fn, reads=(), writes=(), extra=(), dma=None):
        deps = []
        for b in reads:
            if b.w:
                deps.append(b.w)
        wov = []
        for b in writes:
            for o in self._overlaps(b):
                wov.append(o)
                if o.w:
                    deps.append(o.w)
                for s, v in o.r.items():
                    deps.append((s, v))
        deps.extend(e for e in extra if e)
        m = {}
        for s, v in deps:
            if m.get(s, -1) < v:
                m[s] = v
        kn = self.known[eng]
        waits = []
        for s, v in m.items():
            if kn.get(s, 0) >= v:
                continue
            kn[s] = v
            waits.append((s, v))
        sname = self.dsem(dma) if dma else eng
        self.cnt[sname] += 16 if dma else 1
        tok = (sname, self.cnt[sname])
        self.ops[eng].append((waits, fn, (self.sem[sname], 16 if dma else 1)))
        for o in wov:
            o.w = tok
            o.r = {}
        for b in writes:
            b.w = tok
            b.r = {}
        for b in reads:
            if b.r.get(tok[0], -1) < tok[1]:
                b.r[tok[0]] = tok[1]
        return tok

    def raw(self, eng, fn):
        self.ops[eng].append(((), fn, None))

    def replay(self, blk):
        def mk(eng_name):
            sems = self.sem

            def body(h):
                for waits, fn, inc in self.ops[eng_name]:
                    for s, v in waits:
                        h.wait_ge(sems[s], v)
                    ins = fn(h)
                    if inc is not None and ins is not None:
                        ins.then_inc(inc[0], inc[1])
            return body

        blk.sync(mk("sp"))
        blk.tensor(mk("pe"))
        blk.vector(mk("dve"))
        blk.scalar(mk("act"))
        blk.gpsimd(mk("pool"))



def build(n_experts=NEXP, do_attn=True, do_pass2=True, identity_only=False, dbg=None, ldw4=False):
    nc = bass.Bass("TRN2", target_bir_lowering=False)
    ctx = ExitStack()

    d_xin = nc.dram_tensor("xin", [TPC, D], F32, kind="ExternalInput").ap()
    d_out = nc.dram_tensor("out", [TPC, D], F32, kind="ExternalOutput").ap()
    d_dbgh = nc.dram_tensor("dbgh", [128, 16 * TPC], F32, kind="ExternalOutput").ap() if dbg else None
    d_dbgx = nc.dram_tensor("dbgx", [128, 8 * TPC], F32, kind="ExternalOutput").ap() if dbg else None
    d_ws1 = nc.dram_tensor("ws1", [NEXP, 64, 128, 1024], F32, kind="ExternalInput").ap()
    d_ws2 = nc.dram_tensor("ws2", [NEXP, 8, 128, 2048], F32, kind="ExternalInput").ap()
    d_b1h = nc.dram_tensor("b1h", [128, NEXP * 32], F32, kind="ExternalInput").ap()
    d_b3h = nc.dram_tensor("b3h", [128, NEXP * 32], F32, kind="ExternalInput").ap()
    d_gwh = nc.dram_tensor("gwh", [128, 64], F32, kind="ExternalInput").ap()
    d_gbh = nc.dram_tensor("gbh", [1, 8], F32, kind="ExternalInput").ap()
    d_pjh = nc.dram_tensor("pjh", [128, 8192], F32, kind="ExternalInput").ap()
    d_iwh = nc.dram_tensor("iwh", [128, 8192], F32, kind="ExternalInput").ap()
    d_xisfm = nc.dram_tensor("xisfm", [128, 8], F32, kind="ExternalInput").ap()
    d_xisrow = nc.dram_tensor("xisrow", [1, 1024], F32, kind="ExternalInput").ap()
    d_b2 = nc.dram_tensor("b2", [8, 1024], F32, kind="ExternalInput").ap()
    d_sb2fm = nc.dram_tensor("sb2fm", [128, 8], F32, kind="ExternalInput").ap()
    d_ident = nc.dram_tensor("ident", [128, 128], F32, kind="ExternalInput").ap()
    d_onesb = nc.dram_tensor("onesb", [128, 128], BF16, kind="ExternalInput").ap()
    d_selb = nc.dram_tensor("selb", [8, 1024], BF16, kind="ExternalInput").ap()

    P = Prog(nc, ctx)

    BASE = 17408
    TOP = nc.sbuf_top
    cur = [BASE]
    tensors = {}

    def at(name, shape, dtype, offset=None):
        per = int(np.prod(shape[1:])) * mybir.dt.size(dtype)
        if offset is None:
            offset = cur[0]
            cur[0] = (offset + per + 31) & ~31
        assert offset % 32 == 0 and offset + per <= TOP, \
            f"SBUF overflow/misalign: {name} @{offset}+{per} (top {TOP})"
        h = nc.alloc_sbuf_tensor_at(name, shape, dtype, offset=offset)
        tensors[name] = (h, offset, per)
        return h

    def sbuf(name, shape, dtype, offset=None):
        h = at(name, shape, dtype, offset)
        _, o, per = tensors[name]
        b = P.buf(name, "sb", (o, o + per))
        return h, b

    s_x = at("s_x", [128, 8, TPC], F32)
    X_OFF = tensors["s_x"][1]
    s_xln = at("s_xln", [128, 8, TPC], BF16)
    XLN_OFF = tensors["s_xln"][1]
    s_h = at("s_h", [128, 16, TPC], BF16)
    H_OFF = tensors["s_h"][1]
    s_feats = at("s_feats", [128, 8, TPC], BF16, offset=H_OFF)
    s_wrm = at("s_wrm", [128, 8192], BF16, offset=H_OFF + 32768)
    s_pjb = at("s_pjb", [128, 8, 1024], BF16, offset=H_OFF + 49152)
    W_OFF = cur[0]
    s_wr = [at(f"s_wr{k}", [128, 1024], BF16, offset=W_OFF + 2048 * k) for k in range(4)]
    s_w2r = [at(f"s_w2r{k}", [128, 1024], BF16, offset=W_OFF + 8192 + 2048 * k) for k in range(3)]
    cur[0] = W_OFF + 14336
    CF_OFF = cur[0]
    s_combfm = at("s_combfm", [8, TPC], BF16)
    CB_OFF = cur[0]
    s_combbc = at("s_combbc", [128, TPC], BF16)
    AR = cur[0]
    s_cA = [at(f"s_cA{k}", [128, SL], F32, offset=AR + 2048 * k) for k in range(2)]
    s_cB = [at(f"s_cB{k}", [128, SL], F32, offset=AR + 4096 + 2048 * k) for k in range(2)]
    s_gs = [at(f"s_gs{k}", [128, SL], BF16, offset=AR + 8192 + 1024 * k) for k in range(4)]
    s_xb2 = [at(f"s_xb2{k}", [128, TPC], BF16, offset=AR + 4096 * k) for k in range(2)]
    s_xsq2 = [at(f"s_xsq2{k}", [128, TPC], BF16, offset=AR + 8192 + 4096 * k) for k in range(2)]
    s_xb = s_xb2[0]
    s_xsq = s_xsq2[0]
    s_mb = [at(f"s_mb{k}", [128, SL], F32, offset=AR + 8192 + 2048 * k) for k in range(2)]
    s_rstd = [at(f"s_rstd{k}", [128, SL], F32, offset=AR + 12288 + 2048 * k) for k in range(2)]
    s_glog = at("s_glog", [8, TPC], F32, offset=CF_OFF)
    s_io = [at(f"s_io{k}", [128, 1024], F32, offset=AR + 4096 * k) for k in range(4)]
    s_nsq = at("s_nsq", [1, TPC], F32, offset=AR)
    s_att = [at(f"s_att{k}", [128, SL], F32, offset=AR + 8192 + 2048 * k) for k in range(2)]
    s_sqA2 = [at(f"s_sqA2{k}", [128, 1024], BF16, offset=AR + 12288 + 2048 * k) for k in range(2)]
    s_sqA = s_sqA2[0]
    s_niw = at("s_niw", [1, 1024], F32, offset=CF_OFF)
    s_msab = at("s_msab", [1, 1024], F32, offset=CB_OFF)
    cur[0] = AR + 16384
    s_b1 = at("s_b1", [128, NEXP * 32], F32)
    s_b3 = at("s_b3", [128, NEXP * 32], F32)
    s_b2s = at("s_b2s", [8, 1024], BF16)
    s_b2f = at("s_b2f", [8, 1024], F32, offset=H_OFF)
    s_gwb = at("s_gwb", [128, 64], BF16)
    s_gbb = at("s_gbb", [128, 8], F32)
    s_gbf = at("s_gbf", [1, 8], BF16)
    s_xisfm = at("s_xisfm", [128, 8], F32)
    s_ab32 = at("s_ab32", [128, 8], F32)
    s_sb2h = at("s_sb2h", [128, 8], F32)
    s_ident = at("s_ident", [128, 128], F32)
    s_onesb = at("s_onesb", [128, 128], BF16)
    s_selb = at("s_selb", [8, 1024], BF16)
    s_eps = at("s_eps", [128, 1], F32)
    s_ctm = at("s_ctm", [128, 128], F32)
    CTM_OFF = tensors["s_ctm"][1]
    s_gtm = at("s_gtm", [128, 128], F32, offset=CTM_OFF)
    s_gm = at("s_gm", [128, 128], F32, offset=CTM_OFF)
    s_gsc = at("s_gsc", [128, 128], F32)
    s_gw1 = at("s_gw1", [128, 16], F32)
    s_gw2 = at("s_gw2", [128, 16], F32)
    s_geq = at("s_geq", [128, 128], F32)
    assert cur[0] <= TOP, f"SBUF budget exceeded: {cur[0]} > {TOP}"

    def rbuf(name):
        h, o, per = tensors[name]
        return P.buf(name, "sb", (o, o + per))

    bx = [P.buf(f"x{dc}", "sb", (X_OFF + dc * TPC * 4, X_OFF + (dc + 1) * TPC * 4)) for dc in range(8)]
    bxln2 = [[P.buf(f"xln{dc}_{hf}", "sb",
                    (XLN_OFF + dc * TPC * 2 + hf * TPC, XLN_OFF + dc * TPC * 2 + (hf + 1) * TPC))
               for hf in range(2)] for dc in range(8)]
    bxln = [b for pair in bxln2 for b in pair]
    bh = [P.buf(f"h{ic}", "sb", (H_OFF + ic * TPC * 2, H_OFF + (ic + 1) * TPC * 2)) for ic in range(16)]
    bfeats = rbuf("s_feats")
    FE_OFF = tensors["s_feats"][1]
    bfeats2 = [[P.buf(f"feats{rc}_{hf}", "sb",
                      (FE_OFF + rc * TPC * 2 + hf * TPC, FE_OFF + rc * TPC * 2 + (hf + 1) * TPC))
                for hf in range(2)] for rc in range(8)]
    bwrm = rbuf("s_wrm")
    bpjb = rbuf("s_pjb")
    bwr = [rbuf(f"s_wr{k}") for k in range(4)]
    bw2r = [rbuf(f"s_w2r{k}") for k in range(3)]
    bcombfm = rbuf("s_combfm")
    bcombbc = rbuf("s_combbc")
    bcA = [rbuf(f"s_cA{k}") for k in range(2)]
    bcB = [rbuf(f"s_cB{k}") for k in range(2)]
    bgs = [rbuf(f"s_gs{k}") for k in range(4)]
    bxb2 = [rbuf(f"s_xb2{k}") for k in range(2)]
    bxsq2 = [rbuf(f"s_xsq2{k}") for k in range(2)]
    bxb = bxb2[0]
    bxsq = bxsq2[0]
    bmb = [rbuf(f"s_mb{k}") for k in range(2)]
    brstd = [rbuf(f"s_rstd{k}") for k in range(2)]
    bglog = rbuf("s_glog")
    bgtm = rbuf("s_gtm")
    bgsc = rbuf("s_gsc")
    bgw1 = rbuf("s_gw1")
    bgw2 = rbuf("s_gw2")
    bgm = rbuf("s_gm")
    bgeq = rbuf("s_geq")
    bctm = rbuf("s_ctm")
    bio = [rbuf(f"s_io{k}") for k in range(4)]
    bnsq = rbuf("s_nsq")
    NSQ_OFF = tensors["s_nsq"][1]
    bnsq2 = [P.buf(f"nsq{s}", "sb", (NSQ_OFF + s * SL * 4, NSQ_OFF + (s + 1) * SL * 4)) for s in range(4)]
    batt = [rbuf(f"s_att{k}") for k in range(2)]
    bsqA2 = [rbuf(f"s_sqA2{k}") for k in range(2)]
    bsqA = bsqA2[0]
    bniw = rbuf("s_niw")
    bmsab = rbuf("s_msab")
    bb1 = rbuf("s_b1")
    bb3 = rbuf("s_b3")
    bb2s = rbuf("s_b2s")
    bb2f = rbuf("s_b2f")
    bgwb = rbuf("s_gwb")
    bgbb = rbuf("s_gbb")
    bgbf = rbuf("s_gbf")
    bxisfm = rbuf("s_xisfm")
    bab32 = rbuf("s_ab32")
    bsb2h = rbuf("s_sb2h")
    bident = rbuf("s_ident")
    bonesb = rbuf("s_onesb")
    bselb = rbuf("s_selb")
    beps = rbuf("s_eps")

    pb = [ctx.enter_context(nc.psum_tensor(f"pbk{i}", [128, 512], F32)) for i in range(8)]
    bank = [P.buf(f"bank{i}", "ps", (i * 2048, (i + 1) * 2048)) for i in range(8)]

    P.op("sp", lambda h: h.dma_start(out=s_ident.ap(), in_=d_ident), writes=[bident], dma="c_ident")
    P.op("sp", lambda h: h.dma_start(out=s_onesb.ap(), in_=d_onesb), writes=[bonesb], dma="c_onesb")
    P.op("sp", lambda h: h.dma_start(out=s_selb.ap(), in_=d_selb), writes=[bselb], dma="c_selb")
    P.op("sp", lambda h: h.dma_start(out=s_b1.ap(), in_=d_b1h),
         writes=[bb1], dma="c_b1")
    P.op("sp", lambda h: h.dma_start(out=s_b3.ap(), in_=d_b3h),
         writes=[bb3], dma="c_b3")
    P.op("sp", lambda h: h.dma_start(out=s_xisfm.ap(), in_=d_xisfm), writes=[bxisfm], dma="c_xisfm")
    P.op("sp", lambda h: h.dma_start(out=s_b2f.ap(), in_=d_b2), writes=[bb2f], dma="c_b2f")
    P.op("sp", lambda h: h.dma_start(out=s_sb2h.ap(), in_=d_sb2fm), writes=[bsb2h], dma="c_sb2h")
    P.op("pool", lambda h: h.dma_start(out=s_gwb.ap(), in_=d_gwh), writes=[bgwb], dma="c_gwb")
    P.op("pool", lambda h: h.dma_start(out=s_gbf.ap(), in_=d_gbh), writes=[bgbf], dma="c_gbf")

    P.op("act", lambda h: h.activation(s_b2s.ap(), s_b2f.ap(), AF.Copy, scale=C_B2),
         reads=[bb2f], writes=[bb2s])
    P.op("act", lambda h: h.activation(s_sb2h.ap(), s_sb2h.ap(), AF.Copy, scale=0.5),
         reads=[bsb2h], writes=[bsb2h])
    P.op("act", lambda h: h.activation(s_ab32.ap(), s_xisfm.ap(), AF.Exp),
         reads=[bxisfm], writes=[bab32])
    P.op("dve", lambda h: h.tensor_scalar(s_ab32.ap(), s_ab32.ap(), 1.0 / 32.0, None, ALU.mult),
         reads=[bab32], writes=[bab32])
    P.op("dve", lambda h: h.memset(s_eps.ap(), EPS), writes=[beps])
    P.op("pe", lambda h: h.matmul(pb[7].ap()[:, 0:8], s_onesb.ap()[0:1, :], s_gbf.ap(),
                                  start=True, stop=True),
         reads=[bonesb, bgbf], writes=[bank[7]])
    P.op("dve", lambda h: h.tensor_copy(s_gbb.ap(), pb[7].ap()[:, 0:8]),
         reads=[bank[7]], writes=[bgbb])

    pump_q = []
    pumped = [0]
    ring_pos = {"w1": 0, "w2": 0}
    s1_count = [0]
    w2_count = [0]

    def queue_expert(e):
        for c in range(64):
            pump_q.append(("w1", d_ws1[e, c]))
        for j in range(8):
            pump_q.append(("w2", d_ws2[e, j][:, 0:1024]))
            pump_q.append(("w2", d_ws2[e, j][:, 1024:2048]))

    CAP = {"w1": 4, "w2": 3}

    def pump(n=1):
        for _ in range(n):
            if pumped[0] >= len(pump_q):
                return
            kind, src = pump_q[pumped[0]]
            consumed = s1_count[0] if kind == "w1" else w2_count[0]
            if ring_pos[kind] - consumed >= CAP[kind]:
                return
            pumped[0] += 1
            if kind == "w1":
                slot = ring_pos["w1"] % 4
                ring_pos["w1"] += 1
                P.op("pool", lambda h, src=src, slot=slot: h.dma_start(
                    out=s_wr[slot].ap(), in_=src), writes=[bwr[slot]], dma=f"wr{slot}")
            else:
                slot = ring_pos["w2"] % 3
                ring_pos["w2"] += 1
                P.op("pool", lambda h, src=src, slot=slot: h.dma_start(
                    out=s_w2r[slot].ap(), in_=src), writes=[bw2r[slot]], dma=f"w2r{slot}")

    for g in range(16):
        sl = g % 4
        P.op("sp", lambda h, g=g, sl=sl: h.dma_start(
            out=s_io[sl].ap(), in_=d_xin[g * 128:(g + 1) * 128, :]),
            writes=[bio[sl]], dma=f"io{sl}")
        for half in range(2):
            bk_i = (g % 2) * 2 + half

            def tfn(h, sl=sl, half=half, bk_i=bk_i):
                ins = None
                for q in range(4):
                    dc = half * 4 + q
                    ins = h.transpose(pb[bk_i].ap()[:, q * 128:(q + 1) * 128],
                                      s_io[sl].ap()[:, dc * 128:(dc + 1) * 128],
                                      s_ident.ap())
                return ins
            P.op("pe", tfn, reads=[bio[sl], bident], writes=[bank[bk_i]])
            P.op("dve", lambda h, g=g, half=half, bk_i=bk_i: h.tensor_copy(
                s_x.ap()[:, half * 4:(half + 1) * 4, g * 128:(g + 1) * 128],
                pb[bk_i].ap().rearrange("p (a b) -> p a b", b=128)),
                reads=[bank[bk_i]], writes=[bx[half * 4 + q] for q in range(4)])

    if identity_only:
        _emit_out = True
    else:
        _emit_out = True

    def emit_ln():
        for dc in range(8):
            q = dc % 2
            P.op("act", lambda h, dc=dc, q=q: h.activation(
                s_xb2[q].ap(), s_x.ap()[:, dc, :], AF.Copy),
                reads=[bx[dc]], writes=[bxb2[q]])
            P.op("dve", lambda h, q=q: h.tensor_tensor(
                s_xsq2[q].ap(), s_xb2[q].ap(), s_xb2[q].ap(), ALU.mult),
                reads=[bxb2[q]], writes=[bxsq2[q]])

            def mmfn(h, dc=dc, q=q):
                ins = None
                for s in range(NS):
                    ins = h.matmul(pb[s].ap(), s_onesb.ap(),
                                   s_xb2[q].ap()[:, s * SL:(s + 1) * SL],
                                   start=(dc == 0), stop=(dc == 7))
                for s in range(NS):
                    ins = h.matmul(pb[4 + s].ap(), s_onesb.ap(),
                                   s_xsq2[q].ap()[:, s * SL:(s + 1) * SL],
                                   start=(dc == 0), stop=(dc == 7))
                return ins
            P.op("pe", mmfn, reads=[bxb2[q], bxsq2[q], bonesb], writes=bank[0:8])
        for s in range(NS):
            pp = s % 2
            P.op("act", lambda h, s=s, pp=pp: h.activation(
                s_mb[pp].ap(), pb[s].ap(), AF.Copy, scale=1.0 / 1024.0),
                reads=[bank[s]], writes=[bmb[pp]])
            P.op("act", lambda h, s=s, pp=pp: h.activation(
                s_rstd[pp].ap(), pb[4 + s].ap(), AF.Copy, scale=1.0 / 1024.0),
                reads=[bank[4 + s]], writes=[brstd[pp]])
            P.op("act", lambda h, pp=pp: h.activation(s_cA[pp].ap(), s_mb[pp].ap(), AF.Square),
                 reads=[bmb[pp]], writes=[bcA[pp]])
            P.op("dve", lambda h, pp=pp: h.tensor_tensor(
                s_rstd[pp].ap(), s_rstd[pp].ap(), s_cA[pp].ap(), ALU.subtract),
                reads=[brstd[pp], bcA[pp]], writes=[brstd[pp]])
            P.op("act", lambda h, pp=pp: h.activation(s_rstd[pp].ap(), s_rstd[pp].ap(),
                                                      AF.Ln, bias=s_eps.ap()),
                 reads=[brstd[pp], beps], writes=[brstd[pp]])
            P.op("act", lambda h, pp=pp: h.activation(s_rstd[pp].ap(), s_rstd[pp].ap(),
                                                      AF.Exp, scale=-0.5),
                 reads=[brstd[pp]], writes=[brstd[pp]])
            for dc in range(8):
                eng = "dve"
                P.op(eng, lambda h, s=s, pp=pp, dc=dc: h.tensor_tensor(
                    s_cB[pp].ap(), s_x.ap()[:, dc, s * SL:(s + 1) * SL],
                    s_mb[pp].ap(), ALU.subtract),
                    reads=[bx[dc], bmb[pp]], writes=[bcB[pp]])
                P.op(eng, lambda h, s=s, pp=pp, dc=dc: h.tensor_tensor(
                    s_xln.ap()[:, dc, s * SL:(s + 1) * SL], s_cB[pp].ap(),
                    s_rstd[pp].ap(), ALU.mult),
                    reads=[bcB[pp], brstd[pp]], writes=[bxln2[dc][s // 2]])

    def r3(ap_, b=8):
        return ap_.rearrange("p (a b) -> p a b", b=b)

    def glog_s(s):
            def gmm(h, s=s):
                ins = None
                for dc in range(8):
                    ins = h.matmul(pb[s].ap()[0:8, :],
                                   s_gwb.ap()[:, dc * 8:(dc + 1) * 8],
                                   s_xln.ap()[:, dc, s * SL:(s + 1) * SL],
                                   start=(dc == 0), stop=(dc == 7))
                return ins
            P.op("pe", gmm, reads=[bgwb] + [bxln2[dc][s // 2] for dc in range(8)],
                 writes=[bank[s]])
            P.op("act", lambda h, s=s: h.activation(
                s_glog.ap()[:, s * SL:(s + 1) * SL], pb[s].ap()[0:8, :], AF.Copy),
                reads=[bank[s]], writes=[bglog])

    def emit_gate():
        for s in (0, 1):
            glog_s(s)

    def emit_gate_rest():
        for s in (2, 3):
            glog_s(s)
        for g in range(16):
            bk_i = 4 + g % 2
            P.op("pe", lambda h, g=g, bk_i=bk_i: h.transpose(
                pb[bk_i].ap()[:, 0:8], s_glog.ap()[0:8, g * 128:(g + 1) * 128],
                s_ident.ap()[0:8, 0:8]),
                reads=[bglog, bident], writes=[bank[bk_i]])
            P.op("dve", lambda h, g=g, bk_i=bk_i: h.tensor_copy(
                s_gtm.ap()[:, g * 8:(g + 1) * 8], pb[bk_i].ap()[:, 0:8]),
                reads=[bank[bk_i]], writes=[bgtm])
        P.op("dve", lambda h: h.reduce_max(r3(s_gw1.ap(), 1), r3(s_gtm.ap()), axis=AX.X),
             reads=[bgtm], writes=[bgw1])
        P.op("dve", lambda h: h.tensor_tensor(
            r3(s_gsc.ap()), r3(s_gtm.ap()),
            r3(s_gw1.ap(), 1).broadcast_to((128, 16, 8)), ALU.subtract),
            reads=[bgtm, bgw1], writes=[bgsc])
        P.op("act", lambda h: h.activation(s_gsc.ap(), s_gsc.ap(), AF.Exp),
             reads=[bgsc], writes=[bgsc])
        P.op("dve", lambda h: h.reduce_sum(r3(s_gw1.ap(), 1), r3(s_gsc.ap()), axis=AX.X),
             reads=[bgsc], writes=[bgw1])
        P.op("dve", lambda h: h.reciprocal(s_gw1.ap(), s_gw1.ap()),
             reads=[bgw1], writes=[bgw1])
        P.op("dve", lambda h: h.tensor_tensor(
            r3(s_gsc.ap()), r3(s_gsc.ap()),
            r3(s_gw1.ap(), 1).broadcast_to((128, 16, 8)), ALU.mult),
            reads=[bgsc, bgw1], writes=[bgsc])
        P.op("dve", lambda h: h.tensor_tensor(
            r3(s_gm.ap()), r3(s_gsc.ap()),
            r3(s_gbb.ap()).broadcast_to((128, 16, 8)), ALU.add),
            reads=[bgsc, bgbb], writes=[bgm])
        P.op("dve", lambda h: h.reduce_max(r3(s_gw1.ap(), 1), r3(s_gm.ap()), axis=AX.X),
             reads=[bgm], writes=[bgw1])
        P.op("dve", lambda h: h.tensor_tensor(
            r3(s_geq.ap()), r3(s_gm.ap()),
            r3(s_gw1.ap(), 1).broadcast_to((128, 16, 8)), ALU.is_equal),
            reads=[bgm, bgw1], writes=[bgeq])
        P.op("dve", lambda h: h.scalar_tensor_tensor(
            s_gm.ap(), s_geq.ap(), -1e30, s_gm.ap(), ALU.mult, ALU.add),
            reads=[bgeq, bgm], writes=[bgm])
        P.op("dve", lambda h: h.reduce_max(r3(s_gw2.ap(), 1), r3(s_gm.ap()), axis=AX.X),
             reads=[bgm], writes=[bgw2])
        P.op("dve", lambda h: h.tensor_tensor(
            r3(s_gm.ap()), r3(s_gm.ap()),
            r3(s_gw2.ap(), 1).broadcast_to((128, 16, 8)), ALU.is_equal),
            reads=[bgm, bgw2], writes=[bgm])
        P.op("dve", lambda h: h.tensor_tensor(s_geq.ap(), s_geq.ap(), s_gm.ap(), ALU.add),
             reads=[bgeq, bgm], writes=[bgeq])
        P.op("dve", lambda h: h.scalar_tensor_tensor(
            s_ctm.ap(), s_geq.ap(), C_EFF, s_gsc.ap(), ALU.mult, ALU.mult),
            reads=[bgeq, bgsc], writes=[bctm])

    def emit_combfm():
        for g in range(16):
            bk_i = 4 + g // 4
            qq = g % 4
            P.op("pe", lambda h, g=g, bk_i=bk_i, qq=qq: h.transpose(
                pb[bk_i].ap()[0:8, qq * 128:(qq + 1) * 128],
                s_ctm.ap()[:, g * 8:(g + 1) * 8], s_ident.ap()),
                reads=[bctm, bident], writes=[bank[bk_i]])
        for q in range(4):
            P.op("act", lambda h, q=q: h.activation(
                s_combfm.ap()[:, q * 512:(q + 1) * 512], pb[4 + q].ap()[0:8, :], AF.Copy),
                reads=[bank[4 + q]], writes=[bcombfm])

    def emit_stage1(e, post_i0=None):
        for i in range(NI):
            if i == 1 and post_i0 is not None:
                post_i0()
            c0 = s1_count[0]
            slots = [(c0 + k) % 4 for k in range(4)]
            if ldw4:
                units = [
                    ("g3", "ab", slots[0], slots[1]),
                    ("l3", "ab", slots[2], slots[3]),
                ]
            else:
                units = [
                    ("g3", "a", slots[0], slots[1]),
                    ("g3", "b", slots[0], slots[1]),
                    ("l3", "a", slots[2], slots[3]),
                    ("l3", "b", slots[2], slots[3]),
                ]
            for (mat, half, w1slot, w3slot) in units:
                if half == "ab":
                    def mmfn(h, w1slot=w1slot, w3slot=w3slot):
                        ins = None
                        for dc in range(8):
                            lhsT = s_wr[w1slot].ap()[:, dc * 128:(dc + 1) * 128]
                            for s in range(NS):
                                ins = h.matmul(pb[s].ap(), lhsT,
                                               s_xln.ap()[:, dc, s * SL:(s + 1) * SL],
                                               start=(dc == 0), stop=(dc == 7))
                        for dc in range(8):
                            lhsT = s_wr[w3slot].ap()[:, dc * 128:(dc + 1) * 128]
                            for s in range(NS):
                                ins = h.matmul(pb[4 + s].ap(), lhsT,
                                               s_xln.ap()[:, dc, s * SL:(s + 1) * SL],
                                               start=(dc == 0), stop=(dc == 7))
                        return ins
                    P.op("pe", mmfn, reads=[bwr[w1slot], bwr[w3slot]] + bxln,
                         writes=bank[0:8])
                    s1_count[0] += 2
                    pump(2)
                    halves = [("a", 0), ("b", 2)]
                else:
                    s0 = 0 if half == "a" else 2
                    hf = 0 if half == "a" else 1
                    base = 0 if half == "a" else 4
                    p0 = base + 2

                    def mmfn(h, w1slot=w1slot, w3slot=w3slot, s0=s0, base=base):
                        ins = None
                        for dc in range(8):
                            lhsT = s_wr[w1slot].ap()[:, dc * 128:(dc + 1) * 128]
                            for k in range(2):
                                ins = h.matmul(pb[base + k].ap(), lhsT,
                                               s_xln.ap()[:, dc, (s0 + k) * SL:(s0 + k + 1) * SL],
                                               start=(dc == 0), stop=(dc == 7))
                        for dc in range(8):
                            lhsT = s_wr[w3slot].ap()[:, dc * 128:(dc + 1) * 128]
                            for k in range(2):
                                ins = h.matmul(pb[base + 2 + k].ap(), lhsT,
                                               s_xln.ap()[:, dc, (s0 + k) * SL:(s0 + k + 1) * SL],
                                               start=(dc == 0), stop=(dc == 7))
                        return ins
                    P.op("pe", mmfn, reads=[bwr[w1slot], bwr[w3slot]] +
                         [bxln2[dc][hf] for dc in range(8)],
                         writes=[bank[base + q] for q in range(4)])
                    if half == "b":
                        s1_count[0] += 2
                    pump(2)
                    halves = None
                if mat == "g3":
                    col = e * 32 + i
                    if half == "ab":
                        kpairs = [(s, 4 + s, s) for s in range(4)]
                    else:
                        kpairs = [(p0 - 2 + k, p0 + k, s0 + k) for k in range(2)]
                    for (b1k, b3k, s) in kpairs:
                        pp = s % 2
                        P.op("dve", lambda h, pp=pp, b3k=b3k, col=col: h.tensor_scalar(
                            s_cA[pp].ap(), pb[b3k].ap(),
                            s_b3.ap()[:, col:col + 1], None, ALU.add),
                            reads=[bank[b3k], bb3], writes=[bcA[pp]])
                        P.op("dve", lambda h, pp=pp, b1k=b1k, col=col: h.scalar_tensor_tensor(
                            s_cB[pp].ap(), pb[b1k].ap(), s_b1.ap()[:, col:col + 1],
                            s_cA[pp].ap(), ALU.add, ALU.mult),
                            reads=[bank[b1k], bb1, bcA[pp]], writes=[bcB[pp]])
                        P.op("dve", lambda h, pp=pp: h.tensor_scalar(
                            s_cA[pp].ap(), s_cB[pp].ap(), LIMIT, None, ALU.min),
                            reads=[bcB[pp]], writes=[bcA[pp]])
                        gslot = s
                        P.op("act", lambda h, pp=pp, gslot=gslot: h.activation(
                            s_gs[gslot].ap(), s_cA[pp].ap(), AF.Silu, scale=ALPHA),
                            reads=[bcA[pp]], writes=[bgs[gslot]])
                if mat == "l3":
                    coll = e * 32 + i + 16
                    if half == "ab":
                        kpairs = [(s, 4 + s, s) for s in range(4)]
                    else:
                        kpairs = [(p0 - 2 + k, p0 + k, s0 + k) for k in range(2)]
                    for (b1k, b3k, s) in kpairs:
                        pp = s % 2
                        gslot = s
                        P.op("dve", lambda h, pp=pp, b3k=b3k, coll=coll: h.tensor_scalar(
                            s_cA[pp].ap(), pb[b3k].ap(),
                            s_b3.ap()[:, coll:coll + 1], None, ALU.add),
                            reads=[bank[b3k], bb3], writes=[bcA[pp]])
                        P.op("dve", lambda h, pp=pp, b1k=b1k, coll=coll: h.scalar_tensor_tensor(
                            s_cB[pp].ap(), pb[b1k].ap(), s_b1.ap()[:, coll:coll + 1],
                            s_cA[pp].ap(), ALU.add, ALU.mult),
                            reads=[bank[b1k], bb1, bcA[pp]], writes=[bcB[pp]])
                        P.op("dve", lambda h, pp=pp: h.tensor_scalar(
                            s_cA[pp].ap(), s_cB[pp].ap(), -LIMIT, LIMIT, ALU.max, ALU.min),
                            reads=[bcB[pp]], writes=[bcA[pp]])
                        P.op("dve", lambda h, s=s, pp=pp, i=i, gslot=gslot: h.scalar_tensor_tensor(
                            s_h.ap()[:, i, s * SL:(s + 1) * SL], s_cA[pp].ap(), 1.0,
                            s_gs[gslot].ap(), ALU.add, ALU.mult),
                            reads=[bcA[pp], bgs[gslot]], writes=[bh[i]])

    def emit_stage2(e, shared):
        if not shared:
            def cbm(h, e=e):
                ins = None
                for s in range(NS):
                    ins = h.matmul(pb[s].ap(), s_selb.ap()[:, e * 128:(e + 1) * 128],
                                   s_combfm.ap()[:, s * SL:(s + 1) * SL],
                                   start=True, stop=True)
                return ins
            P.op("pe", cbm, reads=[bselb, bcombfm], writes=bank[0:4])
            for s in range(NS):
                P.op("act", lambda h, s=s: h.activation(
                    s_combbc.ap()[:, s * SL:(s + 1) * SL], pb[s].ap(), AF.Copy),
                    reads=[bank[s]], writes=[bcombbc])
        for j in range(8):
            slotA = w2_count[0] % 3
            slotB = (w2_count[0] + 1) % 3
            for half in range(2):
                p0 = (2 * j + half) % 4 * 2
                s0 = half * 2

                def mmfn(h, slotA=slotA, slotB=slotB, p0=p0, s0=s0, shared=shared, j=j):
                    ins = None
                    if shared:
                        for k in range(2):
                            ins = h.matmul(pb[p0 + k].ap(),
                                           s_b2s.ap()[:, j * 128:(j + 1) * 128],
                                           s_combfm.ap()[:, (s0 + k) * SL:(s0 + k + 1) * SL],
                                           start=True, stop=False)
                    for ic in range(16):
                        sw = s_w2r[slotA] if ic < 8 else s_w2r[slotB]
                        lhsT = sw.ap()[:, (ic % 8) * 128:(ic % 8 + 1) * 128]
                        for k in range(2):
                            ins = h.matmul(pb[p0 + k].ap(), lhsT,
                                           s_h.ap()[:, ic, (s0 + k) * SL:(s0 + k + 1) * SL],
                                           start=(ic == 0 and not shared), stop=(ic == 15))
                    return ins
                rd = [bw2r[slotA], bw2r[slotB]] + bh
                if shared:
                    rd += [bb2s, bcombfm]
                P.op("pe", mmfn, reads=rd, writes=[bank[p0], bank[p0 + 1]])
                if half == 1:
                    w2_count[0] += 2
                pump(1)
                for k in range(2):
                    s = s0 + k
                    pp = s % 2
                    if shared:
                        P.op("dve", lambda h, s=s, p0=p0, k=k, j=j: h.scalar_tensor_tensor(
                            s_x.ap()[:, j, s * SL:(s + 1) * SL], pb[p0 + k].ap(), C_SH,
                            s_x.ap()[:, j, s * SL:(s + 1) * SL], ALU.mult, ALU.add),
                            reads=[bank[p0 + k], bx[j]], writes=[bx[j]])
                    else:
                        P.op("dve", lambda h, s=s, pp=pp, p0=p0, k=k: h.tensor_tensor(
                            s_cA[pp].ap(), pb[p0 + k].ap(),
                            s_combbc.ap()[:, s * SL:(s + 1) * SL], ALU.mult),
                            reads=[bank[p0 + k], bcombbc], writes=[bcA[pp]])
                        P.op("dve", lambda h, s=s, pp=pp, j=j: h.tensor_tensor(
                            s_x.ap()[:, j, s * SL:(s + 1) * SL],
                            s_x.ap()[:, j, s * SL:(s + 1) * SL],
                            s_cA[pp].ap(), ALU.add),
                            reads=[bcA[pp], bx[j]], writes=[bx[j]])
                if shared and half == 1 and j < 8:
                    P.op("dve", lambda h, j=j: h.tensor_scalar(
                        s_x.ap()[:, j, :], s_x.ap()[:, j, :],
                        s_sb2h.ap()[:, j:j + 1], None, ALU.add),
                        reads=[bx[j], bsb2h], writes=[bx[j]])
    def emit_ffn():
        emit_gate()
        for e in range(n_experts):
            pump(4)
            def _hook():
                emit_gate_rest()
                emit_combfm()
            emit_stage1(e, post_i0=(_hook if e == 0 else None))
            if dbg == "h" and e == 0:
                for ic in range(16):
                    for hf in range(2):
                        P.op("act", lambda h, ic=ic, hf=hf: h.activation(
                            s_io[0].ap(), s_h.ap()[:, ic, hf * 1024:(hf + 1) * 1024], AF.Copy),
                            reads=[bh[ic]], writes=[bio[0]])
                        P.op("sp", lambda h, ic=ic, hf=hf: h.dma_start(
                            out=d_dbgh[:, ic * TPC + hf * 1024: ic * TPC + (hf + 1) * 1024],
                            in_=s_io[0].ap()), reads=[bio[0]], dma="dbg")
                for dc in range(8):
                    for hf in range(2):
                        P.op("act", lambda h, dc=dc, hf=hf: h.activation(
                            s_io[0].ap(), s_xln.ap()[:, dc, hf * 1024:(hf + 1) * 1024], AF.Copy),
                            reads=[bxln[dc]], writes=[bio[0]])
                        P.op("sp", lambda h, dc=dc, hf=hf: h.dma_start(
                            out=d_dbgx[:, dc * TPC + hf * 1024: dc * TPC + (hf + 1) * 1024],
                            in_=s_io[0].ap()), reads=[bio[0]], dma="dbg")
            emit_stage2(e, shared=(e == 8))

    def emit_attn_prep():
        P.op("pool", lambda h: h.dma_start(
            out=s_pjb.ap().rearrange("p a b -> p (a b)"),
            in_=d_pjh), writes=[bpjb], dma="c_pjb")
        P.op("sp", lambda h: h.dma_start(out=s_msab.ap(), in_=d_xisrow),
             writes=[bmsab], dma="c_msab")
        P.op("dve", lambda h: h.tensor_scalar(s_msab.ap(), s_msab.ap(), -0.5, None, ALU.mult),
             reads=[bmsab], writes=[bmsab])
        iwc = [0]

        def iw_chunk(dc):
            slot = iwc[0] % 4
            iwc[0] += 1
            P.op("pool", lambda h, dc=dc, slot=slot: h.dma_start(
                out=s_wr[slot].ap(), in_=d_iwh[:, dc * 1024:(dc + 1) * 1024]),
                writes=[bwr[slot]], dma=f"wr{slot}")
            return slot

        for dc in range(8):
            slot = iw_chunk(dc)
            q = dc % 2
            P.op("act", lambda h, slot=slot, q=q: h.activation(
                s_sqA2[q].ap(), s_wr[slot].ap(), AF.Square),
                reads=[bwr[slot]], writes=[bsqA2[q]])
            P.op("pe", lambda h, dc=dc, q=q: h.matmul(
                pb[6].ap()[0:1, :], s_onesb.ap()[:, 0:1],
                s_sqA2[q].ap()[:, 0:512],
                start=(dc == 0), stop=(dc == 7)),
                reads=[bsqA2[q], bonesb], writes=[bank[6]])
            P.op("pe", lambda h, dc=dc, q=q: h.matmul(
                pb[7].ap()[0:1, :], s_onesb.ap()[:, 0:1],
                s_sqA2[q].ap()[:, 512:1024],
                start=(dc == 0), stop=(dc == 7)),
                reads=[bsqA2[q], bonesb], writes=[bank[7]])
        P.op("dve", lambda h: h.tensor_copy(s_niw.ap()[:, 0:512], pb[6].ap()[0:1, :]),
             reads=[bank[6]], writes=[bniw])
        P.op("dve", lambda h: h.tensor_copy(s_niw.ap()[:, 512:1024], pb[7].ap()[0:1, :]),
             reads=[bank[7]], writes=[bniw])
        for os in range(2):
            for dc in range(8):
                slot = iw_chunk(dc)

                def wrmm(h, dc=dc, os=os, slot=slot):
                    ins = None
                    for rc in range(8):
                        ins = h.matmul(pb[rc].ap(),
                                       s_pjb.ap()[:, rc, dc * 128:(dc + 1) * 128],
                                       s_wr[slot].ap()[:, os * 512:(os + 1) * 512],
                                       start=(dc == 0), stop=False)
                    return ins
                P.op("pe", wrmm, reads=[bpjb, bwr[slot]],
                     writes=bank[0:8])

            def wcor(h, os=os):
                ins = None
                for rc in range(8):
                    ins = h.matmul(pb[rc].ap(),
                                   s_msab.ap()[:, rc * 128:(rc + 1) * 128],
                                   s_niw.ap()[:, os * 512:(os + 1) * 512],
                                   start=False, stop=True)
                return ins
            P.op("pe", wcor, reads=[bmsab, bniw], writes=bank[0:8])
            for rc in range(8):
                pp = rc % 2
                P.op("act", lambda h, rc=rc, os=os, pp=pp: h.activation(
                    s_att[pp].ap(), pb[rc].ap(), AF.Exp,
                    scale=s_xisfm.ap()[:, rc:rc + 1]),
                    reads=[bank[rc], bxisfm], writes=[batt[pp]])
                P.op("dve", lambda h, rc=rc, os=os, pp=pp: h.tensor_scalar(
                    s_wrm.ap()[:, rc * 1024 + os * 512: rc * 1024 + (os + 1) * 512],
                    s_att[pp].ap(), s_ab32.ap()[:, rc:rc + 1], None, ALU.mult),
                    reads=[batt[pp], bab32], writes=[bwrm])
    def emit_attn_main():
        for s in range(NS):
            for dc in range(8):
                q = dc % 2
                P.op("act", lambda h, dc=dc, s=s, q=q: h.activation(
                    s_sqA2[q].ap()[:, 0:SL], s_xln.ap()[:, dc, s * SL:(s + 1) * SL], AF.Square),
                    reads=[bxln2[dc][s // 2]], writes=[bsqA2[q]])
                P.op("pe", lambda h, s=s, dc=dc, q=q: h.matmul(
                    pb[6].ap()[0:1, :], s_onesb.ap()[:, 0:1], s_sqA2[q].ap()[:, 0:SL],
                    start=(dc == 0), stop=(dc == 7)),
                    reads=[bsqA2[q], bonesb], writes=[bank[6]])
            P.op("dve", lambda h, s=s: h.tensor_copy(
                s_nsq.ap()[:, s * SL:(s + 1) * SL], pb[6].ap()[0:1, :]),
                reads=[bank[6]], writes=[bnsq2[s]])
        for rc in range(8):
            for hf in range(2):
                g0 = ((2 * rc + hf) % 4) * 2
                s0 = hf * 2

                def ftm(h, rc=rc, g0=g0, s0=s0):
                    ins = None
                    for dc in range(8):
                        lhsT = s_pjb.ap()[:, rc, dc * 128:(dc + 1) * 128]
                        for k in range(2):
                            ins = h.matmul(pb[g0 + k].ap(), lhsT,
                                           s_xln.ap()[:, dc, (s0 + k) * SL:(s0 + k + 1) * SL],
                                           start=(dc == 0), stop=False)
                    for k in range(2):
                        ins = h.matmul(pb[g0 + k].ap(),
                                       s_msab.ap()[:, rc * 128:(rc + 1) * 128],
                                       s_nsq.ap()[:, (s0 + k) * SL:(s0 + k + 1) * SL],
                                       start=False, stop=True)
                    return ins
                P.op("pe", ftm, reads=[bpjb, bmsab, bnsq2[s0], bnsq2[s0 + 1]] +
                     [bxln2[dc][hf] for dc in range(8)],
                     writes=[bank[g0], bank[g0 + 1]])
                for k in range(2):
                    s = s0 + k
                    pp = s % 2
                    P.op("act", lambda h, rc=rc, g0=g0, k=k, pp=pp: h.activation(
                        s_att[pp].ap(), pb[g0 + k].ap(), AF.Exp,
                        scale=s_xisfm.ap()[:, rc:rc + 1]),
                        reads=[bank[g0 + k], bxisfm], writes=[batt[pp]])
                    P.op("dve", lambda h, rc=rc, s=s, pp=pp, hf=hf: h.tensor_scalar(
                        s_feats.ap()[:, rc, s * SL:(s + 1) * SL],
                        s_att[pp].ap(), s_ab32.ap()[:, rc:rc + 1], None, ALU.mult),
                        reads=[batt[pp], bab32], writes=[bfeats2[rc][hf]])
        for oc in range(8):
            for hf in range(2):
                g0 = ((2 * oc + hf) % 4) * 2
                s0 = hf * 2

                def aom(h, oc=oc, g0=g0, s0=s0):
                    ins = None
                    for rc in range(8):
                        lhsT = s_wrm.ap()[:, rc * 1024 + oc * 128: rc * 1024 + (oc + 1) * 128]
                        for k in range(2):
                            ins = h.matmul(pb[g0 + k].ap(), lhsT,
                                           s_feats.ap()[:, rc, (s0 + k) * SL:(s0 + k + 1) * SL],
                                           start=(rc == 0), stop=(rc == 7))
                    return ins
                P.op("pe", aom, reads=[bwrm] + [bfeats2[rc][hf] for rc in range(8)],
                     writes=[bank[g0], bank[g0 + 1]])
                for k in range(2):
                    s = s0 + k
                    P.op("dve", lambda h, oc=oc, g0=g0, s=s, k=k: h.tensor_tensor(
                        s_x.ap()[:, oc, s * SL:(s + 1) * SL],
                        s_x.ap()[:, oc, s * SL:(s + 1) * SL],
                        pb[g0 + k].ap(), ALU.add),
                        reads=[bank[g0 + k], bx[oc]], writes=[bx[oc]])

    if not identity_only:
        for e in range(n_experts):
            queue_expert(e)
        pump(4)
        emit_ln()
        emit_ffn()
        if do_attn:
            emit_attn_prep()
            emit_ln()
            emit_attn_main()
        if do_pass2:
            for e in range(n_experts):
                queue_expert(e)
            emit_ln()
            emit_ffn()

    for g in range(16):
        sl = g % 4
        for half in range(2):
            bk_i = (g % 2) * 2 + half

            def tofn(h, g=g, half=half, bk_i=bk_i):
                ins = None
                for q in range(4):
                    dc = half * 4 + q
                    ins = h.transpose(pb[bk_i].ap()[:, q * 128:(q + 1) * 128],
                                      s_x.ap()[:, dc, g * 128:(g + 1) * 128],
                                      s_ident.ap())
                return ins
            P.op("pe", tofn, reads=[bx[half * 4 + q] for q in range(4)] + [bident],
                 writes=[bank[bk_i]])
            if half == 0:
                P.op("dve", lambda h, half=half, bk_i=bk_i, sl=sl: h.tensor_copy(
                    s_io[sl].ap()[:, half * 512:(half + 1) * 512], pb[bk_i].ap()),
                    reads=[bank[bk_i]], writes=[bio[sl]])
            else:
                P.op("act", lambda h, half=half, bk_i=bk_i, sl=sl: h.activation(
                    s_io[sl].ap()[:, half * 512:(half + 1) * 512], pb[bk_i].ap(), AF.Copy),
                    reads=[bank[bk_i]], writes=[bio[sl]])
        P.op("sp", lambda h, g=g, sl=sl: h.dma_start(
            out=d_out[g * 128:(g + 1) * 128, :], in_=s_io[sl].ap()),
            reads=[bio[sl]], writes=[P.buf(f"st{g}")], dma=f"st{sl}")

    for sl in range(4):
        nm = "d_st" + str(sl)
        if nm in P.cnt:
            final = P.cnt[nm]
            P.raw("sp", lambda h, nm=nm, final=final: h.wait_ge(P.sem[nm], final))

    with nc.Block() as blk:
        P.replay(blk)
    ctx.close()
    assert nc.sbuf_base <= BASE, f"bass consts collided with manual map: {nc.sbuf_base} > {BASE}"
    return nc


def kernel(**inputs):
    common, shards = prep_host(inputs)
    nc = build()
    in_maps = []
    for c in range(8):
        m = dict(common)
        m["xin"] = shards[c]
        in_maps.append(m)
    res = run_bass_kernel_spmd(nc, in_maps, core_ids=list(range(8)))
    out = np.concatenate([np.asarray(r["out"]) for r in res.results], axis=0)
    return out.reshape(4, 4096, 1024).astype(np.float32)


# version 12
# speedup vs baseline: 1.0486x; 1.0486x over previous
import numpy as np
import concourse.bass as bass
import concourse.mybir as mybir
from concourse.bass_utils import run_bass_kernel_spmd
from contextlib import ExitStack

F32 = mybir.dt.float32
BF16 = mybir.dt.bfloat16
AF = mybir.ActivationFunctionType
ALU = mybir.AluOpType
AX = mybir.AxisListType

D = 1024
TPC = 2048
NS = 4
SL = 512
NI = 16
NEXP = 9
ALPHA = 1.702
LIMIT = 7.0
C_EFF = 0.5 / ALPHA
C_SH = 0.5 / ALPHA
C_B2 = 2.0 * ALPHA * ALPHA
EPS = 1e-5

ENG = ("pe", "act", "dve", "pool", "sp")



def _perm_gl():
    return np.concatenate([np.arange(0, 4096, 2), np.arange(1, 4096, 2)])


def _chunk_w1(w):
    return np.ascontiguousarray(
        w.reshape(8, 128, 32, 128).transpose(2, 1, 0, 3).reshape(32, 128, 1024))


def _chunk_w2(w):
    return np.ascontiguousarray(
        w.reshape(16, 128, 8, 128).transpose(2, 1, 0, 3).reshape(8, 128, 2048))


def _stage1_stream(w1c, w3c):
    out = np.empty((64, 128, 1024), dtype=np.float32)
    for i in range(16):
        out[4 * i + 0] = w1c[i]
        out[4 * i + 1] = w3c[i]
        out[4 * i + 2] = w1c[i + 16]
        out[4 * i + 3] = w3c[i + 16]
    return out


def prep_host(inputs):
    import ml_dtypes
    g = {k: np.asarray(v) for k, v in inputs.items()}
    perm = _perm_gl()

    x = np.ascontiguousarray(
        g["L0_attn_x"].astype(np.float32, copy=False).reshape(16384, 1024))
    shards = [np.ascontiguousarray(x[c * TPC:(c + 1) * TPC]) for c in range(8)]

    ws1 = np.empty((NEXP, 64, 128, 1024), dtype=np.float32)
    ws2 = np.empty((NEXP, 8, 128, 2048), dtype=np.float32)
    b1h = np.empty((128, NEXP * 32), dtype=np.float32)
    b3h = np.empty((128, NEXP * 32), dtype=np.float32)
    for e in range(8):
        ws1[e] = _stage1_stream(_chunk_w1(g["L0_ffn_w1"][e][:, perm]),
                                _chunk_w1(g["L0_ffn_w3"][e][:, perm]))
        ws2[e] = _chunk_w2(g["L0_ffn_w2"][e])
        b1h[:, e * 32:(e + 1) * 32] = g["L0_ffn_b1"][e][perm].reshape(32, 128).T
        b3h[:, e * 32:(e + 1) * 32] = g["L0_ffn_b3"][e][perm].reshape(32, 128).T
    ws1[8] = _stage1_stream(_chunk_w1(g["L0_ffn_sw1"][:, perm]),
                            _chunk_w1(g["L0_ffn_sw3"][:, perm]))
    ws2[8] = _chunk_w2(g["L0_ffn_sw2"])
    b1h[:, 8 * 32:9 * 32] = g["L0_ffn_sb1"][perm].reshape(32, 128).T
    b3h[:, 8 * 32:9 * 32] = g["L0_ffn_sb3"][perm].reshape(32, 128).T

    gwh = np.ascontiguousarray(
        g["L0_ffn_gate_w"].reshape(8, 8, 128).transpose(2, 1, 0).reshape(128, 64))
    gbh = np.ascontiguousarray(g["L0_ffn_gate_b"].reshape(1, 8).astype(np.float32))

    pjh = np.ascontiguousarray(
        g["L0_attn_proj_matrix"].reshape(8, 128, 8, 128).transpose(3, 0, 2, 1).reshape(128, 8192))
    iwh = np.ascontiguousarray(
        g["L0_attn_input_weights"].reshape(1024, 8, 128).transpose(2, 1, 0).reshape(128, 8192))
    xisfm = np.ascontiguousarray(g["L0_attn_xis"].reshape(8, 128).T)
    xisrow = np.ascontiguousarray(g["L0_attn_xis"].reshape(1, 1024))

    b2 = np.ascontiguousarray(g["L0_ffn_b2"].astype(np.float32))
    sb2fm = np.ascontiguousarray(g["L0_ffn_sb2"].reshape(8, 128).T)

    ident = np.eye(128, dtype=np.float32)
    onesb = np.ones((128, 128), dtype=ml_dtypes.bfloat16)
    selb = np.zeros((8, 1024), dtype=np.float32)
    for e in range(8):
        selb[e, e * 128:(e + 1) * 128] = 1.0
    selb = selb.astype(ml_dtypes.bfloat16)

    common = dict(ws1=ws1, ws2=ws2, b1h=b1h, b3h=b3h, gwh=gwh, gbh=gbh,
                  pjh=pjh, iwh=iwh, xisfm=xisfm, xisrow=xisrow, b2=b2,
                  sb2fm=sb2fm, ident=ident, onesb=onesb, selb=selb)
    return common, shards



class Buf:
    __slots__ = ("name", "rng", "space", "w", "r")

    def __init__(self, name, space=None, rng=None):
        self.name = name
        self.space = space
        self.rng = rng
        self.w = None
        self.r = {}


class Prog:
    def __init__(self, nc, ctx):
        self.nc = nc
        self.ctx = ctx
        self.ops = {e: [] for e in ENG}
        self.sem = {}
        self.cnt = {}
        self.known = {e: {} for e in ENG}
        self.bufs = []
        for name in ("pe", "act", "dve", "pool", "sp"):
            self.sem[name] = ctx.enter_context(nc.semaphore("sem_" + name))
            self.cnt[name] = 0

    def dsem(self, name):
        name = "d_" + name
        if name not in self.sem:
            self.sem[name] = self.ctx.enter_context(self.nc.semaphore(name))
            self.cnt[name] = 0
        return name

    def buf(self, name, space=None, rng=None):
        b = Buf(name, space, rng)
        self.bufs.append(b)
        return b

    def _overlaps(self, b):
        if b.rng is None:
            return [b]
        out = []
        for o in self.bufs:
            if o.space == b.space and o.rng is not None and \
                    o.rng[0] < b.rng[1] and b.rng[0] < o.rng[1]:
                out.append(o)
        return out

    def op(self, eng, fn, reads=(), writes=(), extra=(), dma=None):
        deps = []
        for b in reads:
            if b.w:
                deps.append(b.w)
        wov = []
        for b in writes:
            for o in self._overlaps(b):
                wov.append(o)
                if o.w:
                    deps.append(o.w)
                for s, v in o.r.items():
                    deps.append((s, v))
        deps.extend(e for e in extra if e)
        m = {}
        for s, v in deps:
            if m.get(s, -1) < v:
                m[s] = v
        kn = self.known[eng]
        waits = []
        for s, v in m.items():
            if kn.get(s, 0) >= v:
                continue
            kn[s] = v
            waits.append((s, v))
        sname = self.dsem(dma) if dma else eng
        self.cnt[sname] += 16 if dma else 1
        tok = (sname, self.cnt[sname])
        self.ops[eng].append((waits, fn, (self.sem[sname], 16 if dma else 1)))
        for o in wov:
            o.w = tok
            o.r = {}
        for b in writes:
            b.w = tok
            b.r = {}
        for b in reads:
            if b.r.get(tok[0], -1) < tok[1]:
                b.r[tok[0]] = tok[1]
        return tok

    def raw(self, eng, fn):
        self.ops[eng].append(((), fn, None))

    def replay(self, blk):
        def mk(eng_name):
            sems = self.sem

            def body(h):
                for waits, fn, inc in self.ops[eng_name]:
                    for s, v in waits:
                        h.wait_ge(sems[s], v)
                    ins = fn(h)
                    if inc is not None and ins is not None:
                        ins.then_inc(inc[0], inc[1])
            return body

        blk.sync(mk("sp"))
        blk.tensor(mk("pe"))
        blk.vector(mk("dve"))
        blk.scalar(mk("act"))
        blk.gpsimd(mk("pool"))



def build(n_experts=NEXP, do_attn=True, do_pass2=True, identity_only=False, dbg=None, ldw4=False):
    nc = bass.Bass("TRN2", target_bir_lowering=False)
    ctx = ExitStack()

    d_xin = nc.dram_tensor("xin", [TPC, D], F32, kind="ExternalInput").ap()
    d_out = nc.dram_tensor("out", [TPC, D], F32, kind="ExternalOutput").ap()
    d_dbgh = nc.dram_tensor("dbgh", [128, 16 * TPC], F32, kind="ExternalOutput").ap() if dbg else None
    d_dbgx = nc.dram_tensor("dbgx", [128, 8 * TPC], F32, kind="ExternalOutput").ap() if dbg else None
    d_ws1 = nc.dram_tensor("ws1", [NEXP, 64, 128, 1024], F32, kind="ExternalInput").ap()
    d_ws2 = nc.dram_tensor("ws2", [NEXP, 8, 128, 2048], F32, kind="ExternalInput").ap()
    d_b1h = nc.dram_tensor("b1h", [128, NEXP * 32], F32, kind="ExternalInput").ap()
    d_b3h = nc.dram_tensor("b3h", [128, NEXP * 32], F32, kind="ExternalInput").ap()
    d_gwh = nc.dram_tensor("gwh", [128, 64], F32, kind="ExternalInput").ap()
    d_gbh = nc.dram_tensor("gbh", [1, 8], F32, kind="ExternalInput").ap()
    d_pjh = nc.dram_tensor("pjh", [128, 8192], F32, kind="ExternalInput").ap()
    d_iwh = nc.dram_tensor("iwh", [128, 8192], F32, kind="ExternalInput").ap()
    d_xisfm = nc.dram_tensor("xisfm", [128, 8], F32, kind="ExternalInput").ap()
    d_xisrow = nc.dram_tensor("xisrow", [1, 1024], F32, kind="ExternalInput").ap()
    d_b2 = nc.dram_tensor("b2", [8, 1024], F32, kind="ExternalInput").ap()
    d_sb2fm = nc.dram_tensor("sb2fm", [128, 8], F32, kind="ExternalInput").ap()
    d_ident = nc.dram_tensor("ident", [128, 128], F32, kind="ExternalInput").ap()
    d_onesb = nc.dram_tensor("onesb", [128, 128], BF16, kind="ExternalInput").ap()
    d_selb = nc.dram_tensor("selb", [8, 1024], BF16, kind="ExternalInput").ap()

    P = Prog(nc, ctx)

    BASE = 17408
    TOP = nc.sbuf_top
    cur = [BASE]
    tensors = {}

    def at(name, shape, dtype, offset=None):
        per = int(np.prod(shape[1:])) * mybir.dt.size(dtype)
        if offset is None:
            offset = cur[0]
            cur[0] = (offset + per + 31) & ~31
        assert offset % 32 == 0 and offset + per <= TOP, \
            f"SBUF overflow/misalign: {name} @{offset}+{per} (top {TOP})"
        h = nc.alloc_sbuf_tensor_at(name, shape, dtype, offset=offset)
        tensors[name] = (h, offset, per)
        return h

    def sbuf(name, shape, dtype, offset=None):
        h = at(name, shape, dtype, offset)
        _, o, per = tensors[name]
        b = P.buf(name, "sb", (o, o + per))
        return h, b

    s_x = at("s_x", [128, 8, TPC], F32)
    X_OFF = tensors["s_x"][1]
    s_xln = at("s_xln", [128, 8, TPC], BF16)
    XLN_OFF = tensors["s_xln"][1]
    s_h = at("s_h", [128, 16, TPC], BF16)
    H_OFF = tensors["s_h"][1]
    s_feats = at("s_feats", [128, 8, TPC], BF16, offset=H_OFF)
    s_wrm = at("s_wrm", [128, 8192], BF16, offset=H_OFF + 32768)
    s_pjb = at("s_pjb", [128, 8, 1024], BF16, offset=H_OFF + 49152)
    W_OFF = cur[0]
    s_wr = [at(f"s_wr{k}", [128, 1024], BF16, offset=W_OFF + 2048 * k) for k in range(4)]
    s_w2r = [at(f"s_w2r{k}", [128, 1024], BF16, offset=W_OFF + 8192 + 2048 * k) for k in range(3)]
    cur[0] = W_OFF + 14336
    CF_OFF = cur[0]
    s_combfm = at("s_combfm", [8, TPC], BF16)
    CB_OFF = cur[0]
    s_combbc = at("s_combbc", [128, TPC], BF16)
    AR = cur[0]
    s_cA = [at(f"s_cA{k}", [128, SL], F32, offset=AR + 2048 * k) for k in range(2)]
    s_cB = [at(f"s_cB{k}", [128, SL], F32, offset=AR + 4096 + 2048 * k) for k in range(2)]
    s_gs = [at(f"s_gs{k}", [128, SL], BF16, offset=AR + 8192 + 1024 * k) for k in range(4)]
    s_xb2 = [at(f"s_xb2{k}", [128, TPC], BF16, offset=AR + 4096 * k) for k in range(2)]
    s_xsq2 = [at(f"s_xsq2{k}", [128, TPC], BF16, offset=AR + 8192 + 4096 * k) for k in range(2)]
    s_xb = s_xb2[0]
    s_xsq = s_xsq2[0]
    s_mb = [at(f"s_mb{k}", [128, SL], F32, offset=AR + 8192 + 2048 * k) for k in range(2)]
    s_rstd = [at(f"s_rstd{k}", [128, SL], F32, offset=AR + 12288 + 2048 * k) for k in range(2)]
    s_glog = at("s_glog", [8, TPC], F32, offset=CF_OFF)
    s_io = [at(f"s_io{k}", [128, 1024], F32, offset=AR + 4096 * k) for k in range(4)]
    s_nsq = at("s_nsq", [1, TPC], F32, offset=AR)
    s_att = [at(f"s_att{k}", [128, SL], F32, offset=AR + 8192 + 2048 * k) for k in range(2)]
    s_sqA2 = [at(f"s_sqA2{k}", [128, 1024], BF16, offset=AR + 12288 + 2048 * k) for k in range(2)]
    s_sqA = s_sqA2[0]
    s_niw = at("s_niw", [1, 1024], F32, offset=CF_OFF)
    s_msab = at("s_msab", [1, 1024], F32, offset=CB_OFF)
    cur[0] = AR + 16384
    s_b1 = at("s_b1", [128, NEXP * 32], F32)
    s_b3 = at("s_b3", [128, NEXP * 32], F32)
    s_b2s = at("s_b2s", [8, 1024], BF16)
    s_b2f = at("s_b2f", [8, 1024], F32, offset=H_OFF)
    s_gwb = at("s_gwb", [128, 64], BF16)
    s_gbb = at("s_gbb", [128, 8], F32)
    s_gbf = at("s_gbf", [1, 8], BF16)
    s_xisfm = at("s_xisfm", [128, 8], F32)
    s_ab32 = at("s_ab32", [128, 8], F32)
    s_sb2h = at("s_sb2h", [128, 8], F32)
    s_ident = at("s_ident", [128, 128], F32)
    s_onesb = at("s_onesb", [128, 128], BF16)
    s_selb = at("s_selb", [8, 1024], BF16)
    s_eps = at("s_eps", [128, 1], F32)
    s_ctm = at("s_ctm", [128, 128], F32)
    CTM_OFF = tensors["s_ctm"][1]
    s_gtm = at("s_gtm", [128, 128], F32, offset=CTM_OFF)
    s_gm = at("s_gm", [128, 128], F32, offset=CTM_OFF)
    s_gsc = at("s_gsc", [128, 128], F32)
    s_gw1 = at("s_gw1", [128, 16], F32)
    s_gw2 = at("s_gw2", [128, 16], F32)
    s_geq = at("s_geq", [128, 128], F32)
    assert cur[0] <= TOP, f"SBUF budget exceeded: {cur[0]} > {TOP}"

    def rbuf(name):
        h, o, per = tensors[name]
        return P.buf(name, "sb", (o, o + per))

    bx = [P.buf(f"x{dc}", "sb", (X_OFF + dc * TPC * 4, X_OFF + (dc + 1) * TPC * 4)) for dc in range(8)]
    bxln2 = [[P.buf(f"xln{dc}_{hf}", "sb",
                    (XLN_OFF + dc * TPC * 2 + hf * TPC, XLN_OFF + dc * TPC * 2 + (hf + 1) * TPC))
               for hf in range(2)] for dc in range(8)]
    bxln = [b for pair in bxln2 for b in pair]
    bh = [P.buf(f"h{ic}", "sb", (H_OFF + ic * TPC * 2, H_OFF + (ic + 1) * TPC * 2)) for ic in range(16)]
    bfeats = rbuf("s_feats")
    FE_OFF = tensors["s_feats"][1]
    bfeats2 = [[P.buf(f"feats{rc}_{hf}", "sb",
                      (FE_OFF + rc * TPC * 2 + hf * TPC, FE_OFF + rc * TPC * 2 + (hf + 1) * TPC))
                for hf in range(2)] for rc in range(8)]
    bwrm = rbuf("s_wrm")
    bpjb = rbuf("s_pjb")
    bwr = [rbuf(f"s_wr{k}") for k in range(4)]
    bw2r = [rbuf(f"s_w2r{k}") for k in range(3)]
    bcombfm = rbuf("s_combfm")
    bcombbc = rbuf("s_combbc")
    bcA = [rbuf(f"s_cA{k}") for k in range(2)]
    bcB = [rbuf(f"s_cB{k}") for k in range(2)]
    bgs = [rbuf(f"s_gs{k}") for k in range(4)]
    bxb2 = [rbuf(f"s_xb2{k}") for k in range(2)]
    bxsq2 = [rbuf(f"s_xsq2{k}") for k in range(2)]
    bxb = bxb2[0]
    bxsq = bxsq2[0]
    bmb = [rbuf(f"s_mb{k}") for k in range(2)]
    brstd = [rbuf(f"s_rstd{k}") for k in range(2)]
    bglog = rbuf("s_glog")
    bgtm = rbuf("s_gtm")
    bgsc = rbuf("s_gsc")
    bgw1 = rbuf("s_gw1")
    bgw2 = rbuf("s_gw2")
    bgm = rbuf("s_gm")
    bgeq = rbuf("s_geq")
    bctm = rbuf("s_ctm")
    bio = [rbuf(f"s_io{k}") for k in range(4)]
    bnsq = rbuf("s_nsq")
    NSQ_OFF = tensors["s_nsq"][1]
    bnsq2 = [P.buf(f"nsq{s}", "sb", (NSQ_OFF + s * SL * 4, NSQ_OFF + (s + 1) * SL * 4)) for s in range(4)]
    batt = [rbuf(f"s_att{k}") for k in range(2)]
    bsqA2 = [rbuf(f"s_sqA2{k}") for k in range(2)]
    bsqA = bsqA2[0]
    bniw = rbuf("s_niw")
    bmsab = rbuf("s_msab")
    bb1 = rbuf("s_b1")
    bb3 = rbuf("s_b3")
    bb2s = rbuf("s_b2s")
    bb2f = rbuf("s_b2f")
    bgwb = rbuf("s_gwb")
    bgbb = rbuf("s_gbb")
    bgbf = rbuf("s_gbf")
    bxisfm = rbuf("s_xisfm")
    bab32 = rbuf("s_ab32")
    bsb2h = rbuf("s_sb2h")
    bident = rbuf("s_ident")
    bonesb = rbuf("s_onesb")
    bselb = rbuf("s_selb")
    beps = rbuf("s_eps")

    pb = [ctx.enter_context(nc.psum_tensor(f"pbk{i}", [128, 512], F32)) for i in range(8)]
    bank = [P.buf(f"bank{i}", "ps", (i * 2048, (i + 1) * 2048)) for i in range(8)]

    P.op("sp", lambda h: h.dma_start(out=s_ident.ap(), in_=d_ident), writes=[bident], dma="c_ident")
    P.op("sp", lambda h: h.dma_start(out=s_onesb.ap(), in_=d_onesb), writes=[bonesb], dma="c_onesb")
    P.op("sp", lambda h: h.dma_start(out=s_selb.ap(), in_=d_selb), writes=[bselb], dma="c_selb")
    P.op("sp", lambda h: h.dma_start(out=s_b1.ap(), in_=d_b1h),
         writes=[bb1], dma="c_b1")
    P.op("sp", lambda h: h.dma_start(out=s_b3.ap(), in_=d_b3h),
         writes=[bb3], dma="c_b3")
    P.op("sp", lambda h: h.dma_start(out=s_xisfm.ap(), in_=d_xisfm), writes=[bxisfm], dma="c_xisfm")
    P.op("sp", lambda h: h.dma_start(out=s_b2f.ap(), in_=d_b2), writes=[bb2f], dma="c_b2f")
    P.op("sp", lambda h: h.dma_start(out=s_sb2h.ap(), in_=d_sb2fm), writes=[bsb2h], dma="c_sb2h")
    P.op("pool", lambda h: h.dma_start(out=s_gwb.ap(), in_=d_gwh), writes=[bgwb], dma="c_gwb")
    P.op("pool", lambda h: h.dma_start(out=s_gbf.ap(), in_=d_gbh), writes=[bgbf], dma="c_gbf")

    P.op("act", lambda h: h.activation(s_b2s.ap(), s_b2f.ap(), AF.Copy, scale=C_B2),
         reads=[bb2f], writes=[bb2s])
    P.op("act", lambda h: h.activation(s_sb2h.ap(), s_sb2h.ap(), AF.Copy, scale=0.5),
         reads=[bsb2h], writes=[bsb2h])
    P.op("act", lambda h: h.activation(s_ab32.ap(), s_xisfm.ap(), AF.Exp),
         reads=[bxisfm], writes=[bab32])
    P.op("dve", lambda h: h.tensor_scalar(s_ab32.ap(), s_ab32.ap(), 1.0 / 32.0, None, ALU.mult),
         reads=[bab32], writes=[bab32])
    P.op("dve", lambda h: h.memset(s_eps.ap(), EPS), writes=[beps])
    P.op("pe", lambda h: h.matmul(pb[7].ap()[:, 0:8], s_onesb.ap()[0:1, :], s_gbf.ap(),
                                  start=True, stop=True),
         reads=[bonesb, bgbf], writes=[bank[7]])
    P.op("dve", lambda h: h.tensor_copy(s_gbb.ap(), pb[7].ap()[:, 0:8]),
         reads=[bank[7]], writes=[bgbb])

    pump_q = []
    pumped = [0]
    ring_pos = {"w1": 0, "w2": 0}
    s1_count = [0]
    w2_count = [0]

    def queue_expert(e):
        for c in range(64):
            pump_q.append(("w1", d_ws1[e, c]))
        for j in range(8):
            pump_q.append(("w2", d_ws2[e, j][:, 0:1024]))
            pump_q.append(("w2", d_ws2[e, j][:, 1024:2048]))

    CAP = {"w1": 4, "w2": 3}

    def pump(n=1):
        for _ in range(n):
            if pumped[0] >= len(pump_q):
                return
            kind, src = pump_q[pumped[0]]
            consumed = s1_count[0] if kind == "w1" else w2_count[0]
            if ring_pos[kind] - consumed >= CAP[kind]:
                return
            pumped[0] += 1
            if kind == "w1":
                slot = ring_pos["w1"] % 4
                ring_pos["w1"] += 1
                P.op("pool", lambda h, src=src, slot=slot: h.dma_start(
                    out=s_wr[slot].ap(), in_=src), writes=[bwr[slot]], dma=f"wr{slot}")
            else:
                slot = ring_pos["w2"] % 3
                ring_pos["w2"] += 1
                P.op("pool", lambda h, src=src, slot=slot: h.dma_start(
                    out=s_w2r[slot].ap(), in_=src), writes=[bw2r[slot]], dma=f"w2r{slot}")

    for g in range(16):
        sl = g % 4
        P.op("sp", lambda h, g=g, sl=sl: h.dma_start(
            out=s_io[sl].ap(), in_=d_xin[g * 128:(g + 1) * 128, :]),
            writes=[bio[sl]], dma=f"io{sl}")
        for half in range(2):
            bk_i = (g % 2) * 2 + half

            def tfn(h, sl=sl, half=half, bk_i=bk_i):
                ins = None
                for q in range(4):
                    dc = half * 4 + q
                    ins = h.transpose(pb[bk_i].ap()[:, q * 128:(q + 1) * 128],
                                      s_io[sl].ap()[:, dc * 128:(dc + 1) * 128],
                                      s_ident.ap())
                return ins
            P.op("pe", tfn, reads=[bio[sl], bident], writes=[bank[bk_i]])
            P.op("dve", lambda h, g=g, half=half, bk_i=bk_i: h.tensor_copy(
                s_x.ap()[:, half * 4:(half + 1) * 4, g * 128:(g + 1) * 128],
                pb[bk_i].ap().rearrange("p (a b) -> p a b", b=128)),
                reads=[bank[bk_i]], writes=[bx[half * 4 + q] for q in range(4)])

    if identity_only:
        _emit_out = True
    else:
        _emit_out = True

    def emit_ln():
        for dc in range(8):
            q = dc % 2
            P.op("act", lambda h, dc=dc, q=q: h.activation(
                s_xb2[q].ap(), s_x.ap()[:, dc, :], AF.Copy),
                reads=[bx[dc]], writes=[bxb2[q]])
            P.op("dve", lambda h, q=q: h.tensor_tensor(
                s_xsq2[q].ap(), s_xb2[q].ap(), s_xb2[q].ap(), ALU.mult),
                reads=[bxb2[q]], writes=[bxsq2[q]])

            def mmfn(h, dc=dc, q=q):
                ins = None
                for s in range(NS):
                    ins = h.matmul(pb[s].ap(), s_onesb.ap(),
                                   s_xb2[q].ap()[:, s * SL:(s + 1) * SL],
                                   start=(dc == 0), stop=(dc == 7))
                for s in range(NS):
                    ins = h.matmul(pb[4 + s].ap(), s_onesb.ap(),
                                   s_xsq2[q].ap()[:, s * SL:(s + 1) * SL],
                                   start=(dc == 0), stop=(dc == 7))
                return ins
            P.op("pe", mmfn, reads=[bxb2[q], bxsq2[q], bonesb], writes=bank[0:8])
        for s in range(NS):
            pp = s % 2
            P.op("act", lambda h, s=s, pp=pp: h.activation(
                s_mb[pp].ap(), pb[s].ap(), AF.Copy, scale=1.0 / 1024.0),
                reads=[bank[s]], writes=[bmb[pp]])
            P.op("act", lambda h, s=s, pp=pp: h.activation(
                s_rstd[pp].ap(), pb[4 + s].ap(), AF.Copy, scale=1.0 / 1024.0),
                reads=[bank[4 + s]], writes=[brstd[pp]])
            P.op("act", lambda h, pp=pp: h.activation(s_cA[pp].ap(), s_mb[pp].ap(), AF.Square),
                 reads=[bmb[pp]], writes=[bcA[pp]])
            P.op("dve", lambda h, pp=pp: h.tensor_tensor(
                s_rstd[pp].ap(), s_rstd[pp].ap(), s_cA[pp].ap(), ALU.subtract),
                reads=[brstd[pp], bcA[pp]], writes=[brstd[pp]])
            P.op("act", lambda h, pp=pp: h.activation(s_rstd[pp].ap(), s_rstd[pp].ap(),
                                                      AF.Ln, bias=s_eps.ap()),
                 reads=[brstd[pp], beps], writes=[brstd[pp]])
            P.op("act", lambda h, pp=pp: h.activation(s_rstd[pp].ap(), s_rstd[pp].ap(),
                                                      AF.Exp, scale=-0.5),
                 reads=[brstd[pp]], writes=[brstd[pp]])
            for dc in range(8):
                eng = "dve"
                P.op(eng, lambda h, s=s, pp=pp, dc=dc: h.tensor_tensor(
                    s_cB[pp].ap(), s_x.ap()[:, dc, s * SL:(s + 1) * SL],
                    s_mb[pp].ap(), ALU.subtract),
                    reads=[bx[dc], bmb[pp]], writes=[bcB[pp]])
                P.op(eng, lambda h, s=s, pp=pp, dc=dc: h.tensor_tensor(
                    s_xln.ap()[:, dc, s * SL:(s + 1) * SL], s_cB[pp].ap(),
                    s_rstd[pp].ap(), ALU.mult),
                    reads=[bcB[pp], brstd[pp]], writes=[bxln2[dc][s // 2]])

    def r3(ap_, b=8):
        return ap_.rearrange("p (a b) -> p a b", b=b)

    def glog_s(s):
            def gmm(h, s=s):
                ins = None
                for dc in range(8):
                    ins = h.matmul(pb[s].ap()[0:8, :],
                                   s_gwb.ap()[:, dc * 8:(dc + 1) * 8],
                                   s_xln.ap()[:, dc, s * SL:(s + 1) * SL],
                                   start=(dc == 0), stop=(dc == 7))
                return ins
            P.op("pe", gmm, reads=[bgwb] + [bxln2[dc][s // 2] for dc in range(8)],
                 writes=[bank[s]])
            P.op("act", lambda h, s=s: h.activation(
                s_glog.ap()[:, s * SL:(s + 1) * SL], pb[s].ap()[0:8, :], AF.Copy),
                reads=[bank[s]], writes=[bglog])

    def emit_gate():
        for s in (0, 1):
            glog_s(s)

    def emit_gate_rest():
        for s in (2, 3):
            glog_s(s)
        for g in range(16):
            bk_i = 4 + g % 2
            P.op("pe", lambda h, g=g, bk_i=bk_i: h.transpose(
                pb[bk_i].ap()[:, 0:8], s_glog.ap()[0:8, g * 128:(g + 1) * 128],
                s_ident.ap()[0:8, 0:8]),
                reads=[bglog, bident], writes=[bank[bk_i]])
            P.op("dve", lambda h, g=g, bk_i=bk_i: h.tensor_copy(
                s_gtm.ap()[:, g * 8:(g + 1) * 8], pb[bk_i].ap()[:, 0:8]),
                reads=[bank[bk_i]], writes=[bgtm])
        P.op("dve", lambda h: h.reduce_max(r3(s_gw1.ap(), 1), r3(s_gtm.ap()), axis=AX.X),
             reads=[bgtm], writes=[bgw1])
        P.op("dve", lambda h: h.tensor_tensor(
            r3(s_gsc.ap()), r3(s_gtm.ap()),
            r3(s_gw1.ap(), 1).broadcast_to((128, 16, 8)), ALU.subtract),
            reads=[bgtm, bgw1], writes=[bgsc])
        P.op("act", lambda h: h.activation(s_gsc.ap(), s_gsc.ap(), AF.Exp),
             reads=[bgsc], writes=[bgsc])
        P.op("dve", lambda h: h.reduce_sum(r3(s_gw1.ap(), 1), r3(s_gsc.ap()), axis=AX.X),
             reads=[bgsc], writes=[bgw1])
        P.op("dve", lambda h: h.reciprocal(s_gw1.ap(), s_gw1.ap()),
             reads=[bgw1], writes=[bgw1])
        P.op("dve", lambda h: h.tensor_tensor(
            r3(s_gsc.ap()), r3(s_gsc.ap()),
            r3(s_gw1.ap(), 1).broadcast_to((128, 16, 8)), ALU.mult),
            reads=[bgsc, bgw1], writes=[bgsc])
        P.op("dve", lambda h: h.tensor_tensor(
            r3(s_gm.ap()), r3(s_gsc.ap()),
            r3(s_gbb.ap()).broadcast_to((128, 16, 8)), ALU.add),
            reads=[bgsc, bgbb], writes=[bgm])
        P.op("dve", lambda h: h.reduce_max(r3(s_gw1.ap(), 1), r3(s_gm.ap()), axis=AX.X),
             reads=[bgm], writes=[bgw1])
        P.op("dve", lambda h: h.tensor_tensor(
            r3(s_geq.ap()), r3(s_gm.ap()),
            r3(s_gw1.ap(), 1).broadcast_to((128, 16, 8)), ALU.is_equal),
            reads=[bgm, bgw1], writes=[bgeq])
        P.op("dve", lambda h: h.scalar_tensor_tensor(
            s_gm.ap(), s_geq.ap(), -1e30, s_gm.ap(), ALU.mult, ALU.add),
            reads=[bgeq, bgm], writes=[bgm])
        P.op("dve", lambda h: h.reduce_max(r3(s_gw2.ap(), 1), r3(s_gm.ap()), axis=AX.X),
             reads=[bgm], writes=[bgw2])
        P.op("dve", lambda h: h.tensor_tensor(
            r3(s_gm.ap()), r3(s_gm.ap()),
            r3(s_gw2.ap(), 1).broadcast_to((128, 16, 8)), ALU.is_equal),
            reads=[bgm, bgw2], writes=[bgm])
        P.op("dve", lambda h: h.tensor_tensor(s_geq.ap(), s_geq.ap(), s_gm.ap(), ALU.add),
             reads=[bgeq, bgm], writes=[bgeq])
        P.op("dve", lambda h: h.scalar_tensor_tensor(
            s_ctm.ap(), s_geq.ap(), C_EFF, s_gsc.ap(), ALU.mult, ALU.mult),
            reads=[bgeq, bgsc], writes=[bctm])

    def emit_combfm():
        for g in range(16):
            bk_i = 4 + g // 4
            qq = g % 4
            P.op("pe", lambda h, g=g, bk_i=bk_i, qq=qq: h.transpose(
                pb[bk_i].ap()[0:8, qq * 128:(qq + 1) * 128],
                s_ctm.ap()[:, g * 8:(g + 1) * 8], s_ident.ap()),
                reads=[bctm, bident], writes=[bank[bk_i]])
        for q in range(4):
            P.op("act", lambda h, q=q: h.activation(
                s_combfm.ap()[:, q * 512:(q + 1) * 512], pb[4 + q].ap()[0:8, :], AF.Copy),
                reads=[bank[4 + q]], writes=[bcombfm])

    def emit_stage1(e, post_i0=None):
        for i in range(NI):
            if i == 1 and post_i0 is not None:
                post_i0()
            c0 = s1_count[0]
            slots = [(c0 + k) % 4 for k in range(4)]
            if ldw4:
                units = [
                    ("g3", "ab", slots[0], slots[1]),
                    ("l3", "ab", slots[2], slots[3]),
                ]
            else:
                units = [
                    ("g3", "a", slots[0], slots[1]),
                    ("g3", "b", slots[0], slots[1]),
                    ("l3", "a", slots[2], slots[3]),
                    ("l3", "b", slots[2], slots[3]),
                ]
            for (mat, half, w1slot, w3slot) in units:
                if half == "ab":
                    def mmfn(h, w1slot=w1slot, w3slot=w3slot):
                        ins = None
                        for dc in range(8):
                            lhsT = s_wr[w1slot].ap()[:, dc * 128:(dc + 1) * 128]
                            for s in range(NS):
                                ins = h.matmul(pb[s].ap(), lhsT,
                                               s_xln.ap()[:, dc, s * SL:(s + 1) * SL],
                                               start=(dc == 0), stop=(dc == 7))
                        for dc in range(8):
                            lhsT = s_wr[w3slot].ap()[:, dc * 128:(dc + 1) * 128]
                            for s in range(NS):
                                ins = h.matmul(pb[4 + s].ap(), lhsT,
                                               s_xln.ap()[:, dc, s * SL:(s + 1) * SL],
                                               start=(dc == 0), stop=(dc == 7))
                        return ins
                    P.op("pe", mmfn, reads=[bwr[w1slot], bwr[w3slot]] + bxln,
                         writes=bank[0:8])
                    s1_count[0] += 2
                    pump(2)
                    halves = [("a", 0), ("b", 2)]
                else:
                    s0 = 0 if half == "a" else 2
                    hf = 0 if half == "a" else 1
                    base = 0 if half == "a" else 4
                    p0 = base + 2

                    def mmfn(h, w1slot=w1slot, w3slot=w3slot, s0=s0, base=base):
                        ins = None
                        for dc in range(8):
                            lhsT = s_wr[w1slot].ap()[:, dc * 128:(dc + 1) * 128]
                            for k in range(2):
                                ins = h.matmul(pb[base + k].ap(), lhsT,
                                               s_xln.ap()[:, dc, (s0 + k) * SL:(s0 + k + 1) * SL],
                                               start=(dc == 0), stop=(dc == 7))
                        for dc in range(8):
                            lhsT = s_wr[w3slot].ap()[:, dc * 128:(dc + 1) * 128]
                            for k in range(2):
                                ins = h.matmul(pb[base + 2 + k].ap(), lhsT,
                                               s_xln.ap()[:, dc, (s0 + k) * SL:(s0 + k + 1) * SL],
                                               start=(dc == 0), stop=(dc == 7))
                        return ins
                    P.op("pe", mmfn, reads=[bwr[w1slot], bwr[w3slot]] +
                         [bxln2[dc][hf] for dc in range(8)],
                         writes=[bank[base + q] for q in range(4)])
                    if half == "b":
                        s1_count[0] += 2
                    pump(2)
                    halves = None
                if mat == "g3":
                    col = e * 32 + i
                    if half == "ab":
                        kpairs = [(s, 4 + s, s) for s in range(4)]
                    else:
                        kpairs = [(p0 - 2 + k, p0 + k, s0 + k) for k in range(2)]
                    for (b1k, b3k, s) in kpairs:
                        pp = s % 2
                        P.op("dve", lambda h, pp=pp, b3k=b3k, col=col: h.tensor_scalar(
                            s_cA[pp].ap(), pb[b3k].ap(),
                            s_b3.ap()[:, col:col + 1], None, ALU.add),
                            reads=[bank[b3k], bb3], writes=[bcA[pp]])
                        P.op("dve", lambda h, pp=pp, b1k=b1k, col=col: h.scalar_tensor_tensor(
                            s_cB[pp].ap(), pb[b1k].ap(), s_b1.ap()[:, col:col + 1],
                            s_cA[pp].ap(), ALU.add, ALU.mult),
                            reads=[bank[b1k], bb1, bcA[pp]], writes=[bcB[pp]])
                        P.op("dve", lambda h, pp=pp: h.tensor_scalar(
                            s_cA[pp].ap(), s_cB[pp].ap(), LIMIT, None, ALU.min),
                            reads=[bcB[pp]], writes=[bcA[pp]])
                        gslot = s
                        P.op("act", lambda h, pp=pp, gslot=gslot: h.activation(
                            s_gs[gslot].ap(), s_cA[pp].ap(), AF.Silu, scale=ALPHA),
                            reads=[bcA[pp]], writes=[bgs[gslot]])
                if mat == "l3":
                    coll = e * 32 + i + 16
                    if half == "ab":
                        kpairs = [(s, 4 + s, s) for s in range(4)]
                    else:
                        kpairs = [(p0 - 2 + k, p0 + k, s0 + k) for k in range(2)]
                    for (b1k, b3k, s) in kpairs:
                        pp = s % 2
                        gslot = s
                        P.op("dve", lambda h, pp=pp, b3k=b3k, coll=coll: h.tensor_scalar(
                            s_cA[pp].ap(), pb[b3k].ap(),
                            s_b3.ap()[:, coll:coll + 1], None, ALU.add),
                            reads=[bank[b3k], bb3], writes=[bcA[pp]])
                        P.op("dve", lambda h, pp=pp, b1k=b1k, coll=coll: h.scalar_tensor_tensor(
                            s_cB[pp].ap(), pb[b1k].ap(), s_b1.ap()[:, coll:coll + 1],
                            s_cA[pp].ap(), ALU.add, ALU.mult),
                            reads=[bank[b1k], bb1, bcA[pp]], writes=[bcB[pp]])
                        P.op("dve", lambda h, pp=pp: h.tensor_scalar(
                            s_cA[pp].ap(), s_cB[pp].ap(), -LIMIT, LIMIT, ALU.max, ALU.min),
                            reads=[bcB[pp]], writes=[bcA[pp]])
                        P.op("dve", lambda h, s=s, pp=pp, i=i, gslot=gslot: h.scalar_tensor_tensor(
                            s_h.ap()[:, i, s * SL:(s + 1) * SL], s_cA[pp].ap(), 1.0,
                            s_gs[gslot].ap(), ALU.add, ALU.mult),
                            reads=[bcA[pp], bgs[gslot]], writes=[bh[i]])

    def emit_stage2(e, shared):
        if not shared:
            def cbm(h, e=e):
                ins = None
                for s in range(NS):
                    ins = h.matmul(pb[s].ap(), s_selb.ap()[:, e * 128:(e + 1) * 128],
                                   s_combfm.ap()[:, s * SL:(s + 1) * SL],
                                   start=True, stop=True)
                return ins
            P.op("pe", cbm, reads=[bselb, bcombfm], writes=bank[0:4])
            for s in range(NS):
                P.op("act", lambda h, s=s: h.activation(
                    s_combbc.ap()[:, s * SL:(s + 1) * SL], pb[s].ap(), AF.Copy),
                    reads=[bank[s]], writes=[bcombbc])
        for j in range(8):
            slotA = w2_count[0] % 3
            slotB = (w2_count[0] + 1) % 3
            for half in range(2):
                p0 = (2 * j + half) % 4 * 2
                s0 = half * 2

                def mmfn_lo(h, slotA=slotA, p0=p0, s0=s0, shared=shared, j=j):
                    ins = None
                    if shared:
                        for k in range(2):
                            ins = h.matmul(pb[p0 + k].ap(),
                                           s_b2s.ap()[:, j * 128:(j + 1) * 128],
                                           s_combfm.ap()[:, (s0 + k) * SL:(s0 + k + 1) * SL],
                                           start=True, stop=False)
                    for ic in range(8):
                        lhsT = s_w2r[slotA].ap()[:, ic * 128:(ic + 1) * 128]
                        for k in range(2):
                            ins = h.matmul(pb[p0 + k].ap(), lhsT,
                                           s_h.ap()[:, ic, (s0 + k) * SL:(s0 + k + 1) * SL],
                                           start=(ic == 0 and not shared), stop=False)
                    return ins

                def mmfn_hi(h, slotB=slotB, p0=p0, s0=s0):
                    ins = None
                    for ic in range(8, 16):
                        lhsT = s_w2r[slotB].ap()[:, (ic % 8) * 128:(ic % 8 + 1) * 128]
                        for k in range(2):
                            ins = h.matmul(pb[p0 + k].ap(), lhsT,
                                           s_h.ap()[:, ic, (s0 + k) * SL:(s0 + k + 1) * SL],
                                           start=False, stop=(ic == 15))
                    return ins
                rd = [bw2r[slotA]] + bh[0:8]
                if shared:
                    rd += [bb2s, bcombfm]
                P.op("pe", mmfn_lo, reads=rd, writes=[bank[p0], bank[p0 + 1]])
                P.op("pe", mmfn_hi, reads=[bw2r[slotB]] + bh[8:16],
                     writes=[bank[p0], bank[p0 + 1]])
                if half == 1:
                    w2_count[0] += 2
                pump(1)
                for k in range(2):
                    s = s0 + k
                    pp = s % 2
                    if shared:
                        P.op("dve", lambda h, s=s, p0=p0, k=k, j=j: h.scalar_tensor_tensor(
                            s_x.ap()[:, j, s * SL:(s + 1) * SL], pb[p0 + k].ap(), C_SH,
                            s_x.ap()[:, j, s * SL:(s + 1) * SL], ALU.mult, ALU.add),
                            reads=[bank[p0 + k], bx[j]], writes=[bx[j]])
                    else:
                        P.op("dve", lambda h, s=s, pp=pp, p0=p0, k=k: h.tensor_tensor(
                            s_cA[pp].ap(), pb[p0 + k].ap(),
                            s_combbc.ap()[:, s * SL:(s + 1) * SL], ALU.mult),
                            reads=[bank[p0 + k], bcombbc], writes=[bcA[pp]])
                        P.op("dve", lambda h, s=s, pp=pp, j=j: h.tensor_tensor(
                            s_x.ap()[:, j, s * SL:(s + 1) * SL],
                            s_x.ap()[:, j, s * SL:(s + 1) * SL],
                            s_cA[pp].ap(), ALU.add),
                            reads=[bcA[pp], bx[j]], writes=[bx[j]])
                if shared and half == 1 and j < 8:
                    P.op("dve", lambda h, j=j: h.tensor_scalar(
                        s_x.ap()[:, j, :], s_x.ap()[:, j, :],
                        s_sb2h.ap()[:, j:j + 1], None, ALU.add),
                        reads=[bx[j], bsb2h], writes=[bx[j]])
    def emit_ffn():
        emit_gate()
        for e in range(n_experts):
            pump(4)
            def _hook():
                emit_gate_rest()
                emit_combfm()
            emit_stage1(e, post_i0=(_hook if e == 0 else None))
            if dbg == "h" and e == 0:
                for ic in range(16):
                    for hf in range(2):
                        P.op("act", lambda h, ic=ic, hf=hf: h.activation(
                            s_io[0].ap(), s_h.ap()[:, ic, hf * 1024:(hf + 1) * 1024], AF.Copy),
                            reads=[bh[ic]], writes=[bio[0]])
                        P.op("sp", lambda h, ic=ic, hf=hf: h.dma_start(
                            out=d_dbgh[:, ic * TPC + hf * 1024: ic * TPC + (hf + 1) * 1024],
                            in_=s_io[0].ap()), reads=[bio[0]], dma="dbg")
                for dc in range(8):
                    for hf in range(2):
                        P.op("act", lambda h, dc=dc, hf=hf: h.activation(
                            s_io[0].ap(), s_xln.ap()[:, dc, hf * 1024:(hf + 1) * 1024], AF.Copy),
                            reads=[bxln[dc]], writes=[bio[0]])
                        P.op("sp", lambda h, dc=dc, hf=hf: h.dma_start(
                            out=d_dbgx[:, dc * TPC + hf * 1024: dc * TPC + (hf + 1) * 1024],
                            in_=s_io[0].ap()), reads=[bio[0]], dma="dbg")
            emit_stage2(e, shared=(e == 8))

    def emit_attn_prep():
        P.op("pool", lambda h: h.dma_start(
            out=s_pjb.ap().rearrange("p a b -> p (a b)"),
            in_=d_pjh), writes=[bpjb], dma="c_pjb")
        P.op("sp", lambda h: h.dma_start(out=s_msab.ap(), in_=d_xisrow),
             writes=[bmsab], dma="c_msab")
        P.op("dve", lambda h: h.tensor_scalar(s_msab.ap(), s_msab.ap(), -0.5, None, ALU.mult),
             reads=[bmsab], writes=[bmsab])
        iwc = [0]

        def iw_chunk(dc):
            slot = iwc[0] % 4
            iwc[0] += 1
            P.op("pool", lambda h, dc=dc, slot=slot: h.dma_start(
                out=s_wr[slot].ap(), in_=d_iwh[:, dc * 1024:(dc + 1) * 1024]),
                writes=[bwr[slot]], dma=f"wr{slot}")
            return slot

        for dc in range(8):
            slot = iw_chunk(dc)
            q = dc % 2
            P.op("act", lambda h, slot=slot, q=q: h.activation(
                s_sqA2[q].ap(), s_wr[slot].ap(), AF.Square),
                reads=[bwr[slot]], writes=[bsqA2[q]])
            P.op("pe", lambda h, dc=dc, q=q: h.matmul(
                pb[6].ap()[0:1, :], s_onesb.ap()[:, 0:1],
                s_sqA2[q].ap()[:, 0:512],
                start=(dc == 0), stop=(dc == 7)),
                reads=[bsqA2[q], bonesb], writes=[bank[6]])
            P.op("pe", lambda h, dc=dc, q=q: h.matmul(
                pb[7].ap()[0:1, :], s_onesb.ap()[:, 0:1],
                s_sqA2[q].ap()[:, 512:1024],
                start=(dc == 0), stop=(dc == 7)),
                reads=[bsqA2[q], bonesb], writes=[bank[7]])
        P.op("dve", lambda h: h.tensor_copy(s_niw.ap()[:, 0:512], pb[6].ap()[0:1, :]),
             reads=[bank[6]], writes=[bniw])
        P.op("dve", lambda h: h.tensor_copy(s_niw.ap()[:, 512:1024], pb[7].ap()[0:1, :]),
             reads=[bank[7]], writes=[bniw])
        for os in range(2):
            for dc in range(8):
                slot = iw_chunk(dc)

                def wrmm(h, dc=dc, os=os, slot=slot):
                    ins = None
                    for rc in range(8):
                        ins = h.matmul(pb[rc].ap(),
                                       s_pjb.ap()[:, rc, dc * 128:(dc + 1) * 128],
                                       s_wr[slot].ap()[:, os * 512:(os + 1) * 512],
                                       start=(dc == 0), stop=False)
                    return ins
                P.op("pe", wrmm, reads=[bpjb, bwr[slot]],
                     writes=bank[0:8])

            def wcor(h, os=os):
                ins = None
                for rc in range(8):
                    ins = h.matmul(pb[rc].ap(),
                                   s_msab.ap()[:, rc * 128:(rc + 1) * 128],
                                   s_niw.ap()[:, os * 512:(os + 1) * 512],
                                   start=False, stop=True)
                return ins
            P.op("pe", wcor, reads=[bmsab, bniw], writes=bank[0:8])
            for rc in range(8):
                pp = rc % 2
                P.op("act", lambda h, rc=rc, os=os, pp=pp: h.activation(
                    s_att[pp].ap(), pb[rc].ap(), AF.Exp,
                    scale=s_xisfm.ap()[:, rc:rc + 1]),
                    reads=[bank[rc], bxisfm], writes=[batt[pp]])
                P.op("dve", lambda h, rc=rc, os=os, pp=pp: h.tensor_scalar(
                    s_wrm.ap()[:, rc * 1024 + os * 512: rc * 1024 + (os + 1) * 512],
                    s_att[pp].ap(), s_ab32.ap()[:, rc:rc + 1], None, ALU.mult),
                    reads=[batt[pp], bab32], writes=[bwrm])
    def emit_attn_main():
        for s in range(NS):
            for dc in range(8):
                q = dc % 2
                P.op("act", lambda h, dc=dc, s=s, q=q: h.activation(
                    s_sqA2[q].ap()[:, 0:SL], s_xln.ap()[:, dc, s * SL:(s + 1) * SL], AF.Square),
                    reads=[bxln2[dc][s // 2]], writes=[bsqA2[q]])
                P.op("pe", lambda h, s=s, dc=dc, q=q: h.matmul(
                    pb[6].ap()[0:1, :], s_onesb.ap()[:, 0:1], s_sqA2[q].ap()[:, 0:SL],
                    start=(dc == 0), stop=(dc == 7)),
                    reads=[bsqA2[q], bonesb], writes=[bank[6]])
            P.op("dve", lambda h, s=s: h.tensor_copy(
                s_nsq.ap()[:, s * SL:(s + 1) * SL], pb[6].ap()[0:1, :]),
                reads=[bank[6]], writes=[bnsq2[s]])
        for rc in range(8):
            for hf in range(2):
                g0 = ((2 * rc + hf) % 4) * 2
                s0 = hf * 2

                def ftm(h, rc=rc, g0=g0, s0=s0):
                    ins = None
                    for dc in range(8):
                        lhsT = s_pjb.ap()[:, rc, dc * 128:(dc + 1) * 128]
                        for k in range(2):
                            ins = h.matmul(pb[g0 + k].ap(), lhsT,
                                           s_xln.ap()[:, dc, (s0 + k) * SL:(s0 + k + 1) * SL],
                                           start=(dc == 0), stop=False)
                    for k in range(2):
                        ins = h.matmul(pb[g0 + k].ap(),
                                       s_msab.ap()[:, rc * 128:(rc + 1) * 128],
                                       s_nsq.ap()[:, (s0 + k) * SL:(s0 + k + 1) * SL],
                                       start=False, stop=True)
                    return ins
                P.op("pe", ftm, reads=[bpjb, bmsab, bnsq2[s0], bnsq2[s0 + 1]] +
                     [bxln2[dc][hf] for dc in range(8)],
                     writes=[bank[g0], bank[g0 + 1]])
                for k in range(2):
                    s = s0 + k
                    pp = s % 2
                    P.op("act", lambda h, rc=rc, g0=g0, k=k, pp=pp: h.activation(
                        s_att[pp].ap(), pb[g0 + k].ap(), AF.Exp,
                        scale=s_xisfm.ap()[:, rc:rc + 1]),
                        reads=[bank[g0 + k], bxisfm], writes=[batt[pp]])
                    P.op("dve", lambda h, rc=rc, s=s, pp=pp, hf=hf: h.tensor_scalar(
                        s_feats.ap()[:, rc, s * SL:(s + 1) * SL],
                        s_att[pp].ap(), s_ab32.ap()[:, rc:rc + 1], None, ALU.mult),
                        reads=[batt[pp], bab32], writes=[bfeats2[rc][hf]])
        for oc in range(8):
            for hf in range(2):
                g0 = ((2 * oc + hf) % 4) * 2
                s0 = hf * 2

                def aom(h, oc=oc, g0=g0, s0=s0):
                    ins = None
                    for rc in range(8):
                        lhsT = s_wrm.ap()[:, rc * 1024 + oc * 128: rc * 1024 + (oc + 1) * 128]
                        for k in range(2):
                            ins = h.matmul(pb[g0 + k].ap(), lhsT,
                                           s_feats.ap()[:, rc, (s0 + k) * SL:(s0 + k + 1) * SL],
                                           start=(rc == 0), stop=(rc == 7))
                    return ins
                P.op("pe", aom, reads=[bwrm] + [bfeats2[rc][hf] for rc in range(8)],
                     writes=[bank[g0], bank[g0 + 1]])
                for k in range(2):
                    s = s0 + k
                    P.op("dve", lambda h, oc=oc, g0=g0, s=s, k=k: h.tensor_tensor(
                        s_x.ap()[:, oc, s * SL:(s + 1) * SL],
                        s_x.ap()[:, oc, s * SL:(s + 1) * SL],
                        pb[g0 + k].ap(), ALU.add),
                        reads=[bank[g0 + k], bx[oc]], writes=[bx[oc]])

    if not identity_only:
        for e in range(n_experts):
            queue_expert(e)
        pump(4)
        emit_ln()
        emit_ffn()
        if do_attn:
            emit_attn_prep()
            emit_ln()
            emit_attn_main()
        if do_pass2:
            for e in range(n_experts):
                queue_expert(e)
            emit_ln()
            emit_ffn()

    for g in range(16):
        sl = g % 4
        for half in range(2):
            bk_i = (g % 2) * 2 + half

            def tofn(h, g=g, half=half, bk_i=bk_i):
                ins = None
                for q in range(4):
                    dc = half * 4 + q
                    ins = h.transpose(pb[bk_i].ap()[:, q * 128:(q + 1) * 128],
                                      s_x.ap()[:, dc, g * 128:(g + 1) * 128],
                                      s_ident.ap())
                return ins
            P.op("pe", tofn, reads=[bx[half * 4 + q] for q in range(4)] + [bident],
                 writes=[bank[bk_i]])
            if half == 0:
                P.op("dve", lambda h, half=half, bk_i=bk_i, sl=sl: h.tensor_copy(
                    s_io[sl].ap()[:, half * 512:(half + 1) * 512], pb[bk_i].ap()),
                    reads=[bank[bk_i]], writes=[bio[sl]])
            else:
                P.op("act", lambda h, half=half, bk_i=bk_i, sl=sl: h.activation(
                    s_io[sl].ap()[:, half * 512:(half + 1) * 512], pb[bk_i].ap(), AF.Copy),
                    reads=[bank[bk_i]], writes=[bio[sl]])
        P.op("sp", lambda h, g=g, sl=sl: h.dma_start(
            out=d_out[g * 128:(g + 1) * 128, :], in_=s_io[sl].ap()),
            reads=[bio[sl]], writes=[P.buf(f"st{g}")], dma=f"st{sl}")

    for sl in range(4):
        nm = "d_st" + str(sl)
        if nm in P.cnt:
            final = P.cnt[nm]
            P.raw("sp", lambda h, nm=nm, final=final: h.wait_ge(P.sem[nm], final))

    with nc.Block() as blk:
        P.replay(blk)
    ctx.close()
    assert nc.sbuf_base <= BASE, f"bass consts collided with manual map: {nc.sbuf_base} > {BASE}"
    return nc


def kernel(**inputs):
    common, shards = prep_host(inputs)
    nc = build()
    in_maps = []
    for c in range(8):
        m = dict(common)
        m["xin"] = shards[c]
        in_maps.append(m)
    res = run_bass_kernel_spmd(nc, in_maps, core_ids=list(range(8)))
    out = np.concatenate([np.asarray(r["out"]) for r in res.results], axis=0)
    return out.reshape(4, 4096, 1024).astype(np.float32)


# version 13
# speedup vs baseline: 1.1926x; 1.1373x over previous
import numpy as np
import concourse.bass as bass
import concourse.mybir as mybir
from concourse.bass_utils import run_bass_kernel_spmd
from contextlib import ExitStack

F32 = mybir.dt.float32
BF16 = mybir.dt.bfloat16
AF = mybir.ActivationFunctionType
ALU = mybir.AluOpType
AX = mybir.AxisListType

D = 1024
TPC = 2048
NS = 4
SL = 512
NI = 16
NEXP = 9
ALPHA = 1.702
LIMIT = 7.0
C_EFF = 0.5 / ALPHA
C_SH = 0.5 / ALPHA
C_B2 = 2.0 * ALPHA * ALPHA
EPS = 1e-5

ENG = ("pe", "act", "dve", "pool", "sp")



def _perm_gl():
    return np.concatenate([np.arange(0, 4096, 2), np.arange(1, 4096, 2)])


def _chunk_w1(w):
    return np.ascontiguousarray(
        w.reshape(8, 128, 32, 128).transpose(2, 1, 0, 3).reshape(32, 128, 1024))


def _chunk_w2(w):
    return np.ascontiguousarray(
        w.reshape(16, 128, 8, 128).transpose(2, 1, 0, 3).reshape(8, 128, 2048))


def _stage1_stream(w1c, w3c):
    out = np.empty((64, 128, 1024), dtype=np.float32)
    for i in range(16):
        out[4 * i + 0] = w1c[i]
        out[4 * i + 1] = w3c[i]
        out[4 * i + 2] = w1c[i + 16]
        out[4 * i + 3] = w3c[i + 16]
    return out


def prep_host(inputs):
    import ml_dtypes
    g = {k: np.asarray(v) for k, v in inputs.items()}
    perm = _perm_gl()

    x = np.ascontiguousarray(
        g["L0_attn_x"].astype(np.float32, copy=False).reshape(16384, 1024))
    shards = [np.ascontiguousarray(x[c * TPC:(c + 1) * TPC]) for c in range(8)]

    ws1 = np.empty((NEXP, 64, 128, 1024), dtype=np.float32)
    ws2 = np.empty((NEXP, 8, 128, 2048), dtype=np.float32)
    b1h = np.empty((128, NEXP * 32), dtype=np.float32)
    b3h = np.empty((128, NEXP * 32), dtype=np.float32)
    for e in range(8):
        ws1[e] = _stage1_stream(_chunk_w1(g["L0_ffn_w1"][e][:, perm]),
                                _chunk_w1(g["L0_ffn_w3"][e][:, perm]))
        ws2[e] = _chunk_w2(g["L0_ffn_w2"][e])
        b1h[:, e * 32:(e + 1) * 32] = g["L0_ffn_b1"][e][perm].reshape(32, 128).T
        b3h[:, e * 32:(e + 1) * 32] = g["L0_ffn_b3"][e][perm].reshape(32, 128).T
    ws1[8] = _stage1_stream(_chunk_w1(g["L0_ffn_sw1"][:, perm]),
                            _chunk_w1(g["L0_ffn_sw3"][:, perm]))
    ws2[8] = _chunk_w2(g["L0_ffn_sw2"])
    b1h[:, 8 * 32:9 * 32] = g["L0_ffn_sb1"][perm].reshape(32, 128).T
    b3h[:, 8 * 32:9 * 32] = g["L0_ffn_sb3"][perm].reshape(32, 128).T

    gwh = np.ascontiguousarray(
        g["L0_ffn_gate_w"].reshape(8, 8, 128).transpose(2, 1, 0).reshape(128, 64))
    gbh = np.ascontiguousarray(g["L0_ffn_gate_b"].reshape(1, 8).astype(np.float32))

    pjh = np.ascontiguousarray(
        g["L0_attn_proj_matrix"].reshape(8, 128, 8, 128).transpose(3, 0, 2, 1).reshape(128, 8192))
    iwh = np.ascontiguousarray(
        g["L0_attn_input_weights"].reshape(1024, 8, 128).transpose(2, 1, 0).reshape(128, 8192))
    xisfm = np.ascontiguousarray(g["L0_attn_xis"].reshape(8, 128).T)
    xisrow = np.ascontiguousarray(g["L0_attn_xis"].reshape(1, 1024))

    b2 = np.ascontiguousarray(g["L0_ffn_b2"].astype(np.float32))
    sb2fm = np.ascontiguousarray(g["L0_ffn_sb2"].reshape(8, 128).T)

    ident = np.eye(128, dtype=np.float32)
    onesb = np.ones((128, 128), dtype=ml_dtypes.bfloat16)
    selb = np.zeros((8, 1024), dtype=np.float32)
    for e in range(8):
        selb[e, e * 128:(e + 1) * 128] = 1.0
    selb = selb.astype(ml_dtypes.bfloat16)

    common = dict(ws1=ws1, ws2=ws2, b1h=b1h, b3h=b3h, gwh=gwh, gbh=gbh,
                  pjh=pjh, iwh=iwh, xisfm=xisfm, xisrow=xisrow, b2=b2,
                  sb2fm=sb2fm, ident=ident, onesb=onesb, selb=selb)
    return common, shards



class Buf:
    __slots__ = ("name", "rng", "space", "w", "r")

    def __init__(self, name, space=None, rng=None):
        self.name = name
        self.space = space
        self.rng = rng
        self.w = None
        self.r = {}


class Prog:
    def __init__(self, nc, ctx):
        self.nc = nc
        self.ctx = ctx
        self.ops = {e: [] for e in ENG}
        self.sem = {}
        self.cnt = {}
        self.known = {e: {} for e in ENG}
        self.bufs = []
        for name in ("pe", "act", "dve", "pool", "sp"):
            self.sem[name] = ctx.enter_context(nc.semaphore("sem_" + name))
            self.cnt[name] = 0

    def dsem(self, name):
        name = "d_" + name
        if name not in self.sem:
            self.sem[name] = self.ctx.enter_context(self.nc.semaphore(name))
            self.cnt[name] = 0
        return name

    def buf(self, name, space=None, rng=None):
        b = Buf(name, space, rng)
        self.bufs.append(b)
        return b

    def _overlaps(self, b):
        if b.rng is None:
            return [b]
        out = []
        for o in self.bufs:
            if o.space == b.space and o.rng is not None and \
                    o.rng[0] < b.rng[1] and b.rng[0] < o.rng[1]:
                out.append(o)
        return out

    def op(self, eng, fn, reads=(), writes=(), extra=(), dma=None):
        deps = []
        for b in reads:
            if b.w:
                deps.append(b.w)
        wov = []
        for b in writes:
            for o in self._overlaps(b):
                wov.append(o)
                if o.w:
                    deps.append(o.w)
                for s, v in o.r.items():
                    deps.append((s, v))
        deps.extend(e for e in extra if e)
        m = {}
        for s, v in deps:
            if m.get(s, -1) < v:
                m[s] = v
        kn = self.known[eng]
        waits = []
        for s, v in m.items():
            if kn.get(s, 0) >= v:
                continue
            kn[s] = v
            waits.append((s, v))
        sname = self.dsem(dma) if dma else eng
        self.cnt[sname] += 16 if dma else 1
        tok = (sname, self.cnt[sname])
        self.ops[eng].append((waits, fn, (self.sem[sname], 16 if dma else 1)))
        for o in wov:
            o.w = tok
            o.r = {}
        for b in writes:
            b.w = tok
            b.r = {}
        for b in reads:
            if b.r.get(tok[0], -1) < tok[1]:
                b.r[tok[0]] = tok[1]
        return tok

    def raw(self, eng, fn):
        self.ops[eng].append(((), fn, None))

    def replay(self, blk):
        def mk(eng_name):
            sems = self.sem

            def body(h):
                for waits, fn, inc in self.ops[eng_name]:
                    for s, v in waits:
                        h.wait_ge(sems[s], v)
                    ins = fn(h)
                    if inc is not None and ins is not None:
                        ins.then_inc(inc[0], inc[1])
            return body

        blk.sync(mk("sp"))
        blk.tensor(mk("pe"))
        blk.vector(mk("dve"))
        blk.scalar(mk("act"))
        blk.gpsimd(mk("pool"))



def build(n_experts=NEXP, do_attn=True, do_pass2=True, identity_only=False, dbg=None, ldw4=False):
    nc = bass.Bass("TRN2", target_bir_lowering=False)
    ctx = ExitStack()

    d_xin = nc.dram_tensor("xin", [TPC, D], F32, kind="ExternalInput").ap()
    d_out = nc.dram_tensor("out", [TPC, D], F32, kind="ExternalOutput").ap()
    d_dbgh = nc.dram_tensor("dbgh", [128, 16 * TPC], F32, kind="ExternalOutput").ap() if dbg else None
    d_dbgx = nc.dram_tensor("dbgx", [128, 8 * TPC], F32, kind="ExternalOutput").ap() if dbg else None
    d_ws1 = nc.dram_tensor("ws1", [NEXP, 64, 128, 1024], F32, kind="ExternalInput").ap()
    d_ws2 = nc.dram_tensor("ws2", [NEXP, 8, 128, 2048], F32, kind="ExternalInput").ap()
    d_b1h = nc.dram_tensor("b1h", [128, NEXP * 32], F32, kind="ExternalInput").ap()
    d_b3h = nc.dram_tensor("b3h", [128, NEXP * 32], F32, kind="ExternalInput").ap()
    d_gwh = nc.dram_tensor("gwh", [128, 64], F32, kind="ExternalInput").ap()
    d_gbh = nc.dram_tensor("gbh", [1, 8], F32, kind="ExternalInput").ap()
    d_pjh = nc.dram_tensor("pjh", [128, 8192], F32, kind="ExternalInput").ap()
    d_iwh = nc.dram_tensor("iwh", [128, 8192], F32, kind="ExternalInput").ap()
    d_xisfm = nc.dram_tensor("xisfm", [128, 8], F32, kind="ExternalInput").ap()
    d_xisrow = nc.dram_tensor("xisrow", [1, 1024], F32, kind="ExternalInput").ap()
    d_b2 = nc.dram_tensor("b2", [8, 1024], F32, kind="ExternalInput").ap()
    d_sb2fm = nc.dram_tensor("sb2fm", [128, 8], F32, kind="ExternalInput").ap()
    d_ident = nc.dram_tensor("ident", [128, 128], F32, kind="ExternalInput").ap()
    d_onesb = nc.dram_tensor("onesb", [128, 128], BF16, kind="ExternalInput").ap()
    d_selb = nc.dram_tensor("selb", [8, 1024], BF16, kind="ExternalInput").ap()

    P = Prog(nc, ctx)

    BASE = 17408
    TOP = nc.sbuf_top
    cur = [BASE]
    tensors = {}

    def at(name, shape, dtype, offset=None):
        per = int(np.prod(shape[1:])) * mybir.dt.size(dtype)
        if offset is None:
            offset = cur[0]
            cur[0] = (offset + per + 31) & ~31
        assert offset % 32 == 0 and offset + per <= TOP, \
            f"SBUF overflow/misalign: {name} @{offset}+{per} (top {TOP})"
        h = nc.alloc_sbuf_tensor_at(name, shape, dtype, offset=offset)
        tensors[name] = (h, offset, per)
        return h

    def sbuf(name, shape, dtype, offset=None):
        h = at(name, shape, dtype, offset)
        _, o, per = tensors[name]
        b = P.buf(name, "sb", (o, o + per))
        return h, b

    s_x = at("s_x", [128, 8, TPC], F32)
    X_OFF = tensors["s_x"][1]
    s_xln = at("s_xln", [128, 8, TPC], BF16)
    XLN_OFF = tensors["s_xln"][1]
    s_h = at("s_h", [128, 16, TPC], BF16)
    H_OFF = tensors["s_h"][1]
    s_feats = at("s_feats", [128, 8, TPC], BF16, offset=H_OFF)
    s_wrm = at("s_wrm", [128, 8192], BF16, offset=H_OFF + 32768)
    s_pjb = at("s_pjb", [128, 8, 1024], BF16, offset=H_OFF + 49152)
    W_OFF = cur[0]
    s_wr = [at(f"s_wr{k}", [128, 1024], BF16, offset=W_OFF + 2048 * k) for k in range(4)]
    s_w2r = [at(f"s_w2r{k}", [128, 1024], BF16, offset=W_OFF + 8192 + 2048 * k) for k in range(3)]
    cur[0] = W_OFF + 14336
    CF_OFF = cur[0]
    s_combfm = at("s_combfm", [8, TPC], BF16)
    CB_OFF = cur[0]
    s_combbc = at("s_combbc", [128, TPC], BF16)
    AR = cur[0]
    s_cA = [at(f"s_cA{k}", [128, SL], F32, offset=AR + 2048 * k) for k in range(2)]
    s_cB = [at(f"s_cB{k}", [128, SL], F32, offset=AR + 4096 + 2048 * k) for k in range(2)]
    s_gs = [at(f"s_gs{k}", [128, SL], BF16, offset=AR + 8192 + 1024 * k) for k in range(4)]
    s_xb2 = [at(f"s_xb2{k}", [128, TPC], BF16, offset=AR + 4096 * k) for k in range(2)]
    s_xsq2 = [at(f"s_xsq2{k}", [128, TPC], BF16, offset=AR + 8192 + 4096 * k) for k in range(2)]
    s_xb = s_xb2[0]
    s_xsq = s_xsq2[0]
    s_mb = [at(f"s_mb{k}", [128, SL], F32, offset=AR + 8192 + 2048 * k) for k in range(2)]
    s_rstd = [at(f"s_rstd{k}", [128, SL], F32, offset=AR + 12288 + 2048 * k) for k in range(2)]
    s_glog = at("s_glog", [8, TPC], F32, offset=CF_OFF)
    s_io = [at(f"s_io{k}", [128, 1024], F32, offset=AR + 4096 * k) for k in range(4)]
    s_nsq = at("s_nsq", [1, TPC], F32, offset=AR)
    s_nsqA = at("s_nsqA", [1, 1024], F32, offset=CF_OFF)
    s_nsqB = at("s_nsqB", [1, 1024], F32, offset=W_OFF)
    s_att = [at(f"s_att{k}", [128, SL], F32, offset=AR + 8192 + 2048 * k) for k in range(2)]
    s_sqA2 = [at(f"s_sqA2{k}", [128, 1024], BF16, offset=AR + 12288 + 2048 * k) for k in range(2)]
    s_sqA = s_sqA2[0]
    s_niw = at("s_niw", [1, 1024], F32, offset=CF_OFF)
    s_msab = at("s_msab", [1, 1024], F32, offset=CB_OFF)
    cur[0] = AR + 16384
    s_b1 = at("s_b1", [128, NEXP * 32], F32)
    s_b3 = at("s_b3", [128, NEXP * 32], F32)
    s_b2s = at("s_b2s", [8, 1024], BF16)
    s_b2f = at("s_b2f", [8, 1024], F32, offset=H_OFF)
    s_gwb = at("s_gwb", [128, 64], BF16)
    s_gbb = at("s_gbb", [128, 8], F32)
    s_gbf = at("s_gbf", [1, 8], BF16)
    s_xisfm = at("s_xisfm", [128, 8], F32)
    s_ab32 = at("s_ab32", [128, 8], F32)
    s_sb2h = at("s_sb2h", [128, 8], F32)
    s_ident = at("s_ident", [128, 128], F32)
    s_onesb = at("s_onesb", [128, 128], BF16)
    s_selb = at("s_selb", [8, 1024], BF16)
    s_eps = at("s_eps", [128, 1], F32)
    s_ctm = at("s_ctm", [128, 128], F32)
    CTM_OFF = tensors["s_ctm"][1]
    s_gtm = at("s_gtm", [128, 128], F32, offset=CTM_OFF)
    s_gm = at("s_gm", [128, 128], F32, offset=CTM_OFF)
    s_gsc = at("s_gsc", [128, 128], F32)
    s_gw1 = at("s_gw1", [128, 16], F32)
    s_gw2 = at("s_gw2", [128, 16], F32)
    s_geq = at("s_geq", [128, 128], F32)
    assert cur[0] <= TOP, f"SBUF budget exceeded: {cur[0]} > {TOP}"

    def rbuf(name):
        h, o, per = tensors[name]
        return P.buf(name, "sb", (o, o + per))

    bx = [P.buf(f"x{dc}", "sb", (X_OFF + dc * TPC * 4, X_OFF + (dc + 1) * TPC * 4)) for dc in range(8)]
    bxln2 = [[P.buf(f"xln{dc}_{hf}", "sb",
                    (XLN_OFF + dc * TPC * 2 + hf * TPC, XLN_OFF + dc * TPC * 2 + (hf + 1) * TPC))
               for hf in range(2)] for dc in range(8)]
    bxln = [b for pair in bxln2 for b in pair]
    bh = [P.buf(f"h{ic}", "sb", (H_OFF + ic * TPC * 2, H_OFF + (ic + 1) * TPC * 2)) for ic in range(16)]
    bfeats = rbuf("s_feats")
    FE_OFF = tensors["s_feats"][1]
    bfeats2 = [[P.buf(f"feats{rc}_{hf}", "sb",
                      (FE_OFF + rc * TPC * 2 + hf * TPC, FE_OFF + rc * TPC * 2 + (hf + 1) * TPC))
                for hf in range(2)] for rc in range(8)]
    bwrm = rbuf("s_wrm")
    bpjb = rbuf("s_pjb")
    bwr = [rbuf(f"s_wr{k}") for k in range(4)]
    bw2r = [rbuf(f"s_w2r{k}") for k in range(3)]
    bcombfm = rbuf("s_combfm")
    bcombbc = rbuf("s_combbc")
    bcA = [rbuf(f"s_cA{k}") for k in range(2)]
    bcB = [rbuf(f"s_cB{k}") for k in range(2)]
    bgs = [rbuf(f"s_gs{k}") for k in range(4)]
    bxb2 = [rbuf(f"s_xb2{k}") for k in range(2)]
    bxsq2 = [rbuf(f"s_xsq2{k}") for k in range(2)]
    bxb = bxb2[0]
    bxsq = bxsq2[0]
    bmb = [rbuf(f"s_mb{k}") for k in range(2)]
    brstd = [rbuf(f"s_rstd{k}") for k in range(2)]
    bglog = rbuf("s_glog")
    bgtm = rbuf("s_gtm")
    bgsc = rbuf("s_gsc")
    bgw1 = rbuf("s_gw1")
    bgw2 = rbuf("s_gw2")
    bgm = rbuf("s_gm")
    bgeq = rbuf("s_geq")
    bctm = rbuf("s_ctm")
    bio = [rbuf(f"s_io{k}") for k in range(4)]
    NSQA_OFF = tensors["s_nsqA"][1]
    NSQB_OFF = tensors["s_nsqB"][1]
    bnsq2 = [P.buf(f"nsq{s}", "sb",
                   ((NSQA_OFF if s < 2 else NSQB_OFF) + (s % 2) * SL * 4,
                    (NSQA_OFF if s < 2 else NSQB_OFF) + (s % 2 + 1) * SL * 4)) for s in range(4)]

    def nsq_row(s):
        t = s_nsqA if s < 2 else s_nsqB
        return t.ap()[0:1, (s % 2) * SL:(s % 2 + 1) * SL]
    batt = [rbuf(f"s_att{k}") for k in range(2)]
    bsqA2 = [rbuf(f"s_sqA2{k}") for k in range(2)]
    bsqA = bsqA2[0]
    bniw = rbuf("s_niw")
    bmsab = rbuf("s_msab")
    bb1 = rbuf("s_b1")
    bb3 = rbuf("s_b3")
    bb2s = rbuf("s_b2s")
    bb2f = rbuf("s_b2f")
    bgwb = rbuf("s_gwb")
    bgbb = rbuf("s_gbb")
    bgbf = rbuf("s_gbf")
    bxisfm = rbuf("s_xisfm")
    bab32 = rbuf("s_ab32")
    bsb2h = rbuf("s_sb2h")
    bident = rbuf("s_ident")
    bonesb = rbuf("s_onesb")
    bselb = rbuf("s_selb")
    beps = rbuf("s_eps")

    pb = [ctx.enter_context(nc.psum_tensor(f"pbk{i}", [128, 512], F32)) for i in range(8)]
    bank = [P.buf(f"bank{i}", "ps", (i * 2048, (i + 1) * 2048)) for i in range(8)]

    P.op("sp", lambda h: h.dma_start(out=s_ident.ap(), in_=d_ident), writes=[bident], dma="c_ident")
    P.op("sp", lambda h: h.dma_start(out=s_onesb.ap(), in_=d_onesb), writes=[bonesb], dma="c_onesb")
    P.op("sp", lambda h: h.dma_start(out=s_selb.ap(), in_=d_selb), writes=[bselb], dma="c_selb")
    P.op("sp", lambda h: h.dma_start(out=s_b1.ap(), in_=d_b1h),
         writes=[bb1], dma="c_b1")
    P.op("sp", lambda h: h.dma_start(out=s_b3.ap(), in_=d_b3h),
         writes=[bb3], dma="c_b3")
    P.op("sp", lambda h: h.dma_start(out=s_xisfm.ap(), in_=d_xisfm), writes=[bxisfm], dma="c_xisfm")
    P.op("sp", lambda h: h.dma_start(out=s_b2f.ap(), in_=d_b2), writes=[bb2f], dma="c_b2f")
    P.op("sp", lambda h: h.dma_start(out=s_sb2h.ap(), in_=d_sb2fm), writes=[bsb2h], dma="c_sb2h")
    P.op("pool", lambda h: h.dma_start(out=s_gwb.ap(), in_=d_gwh), writes=[bgwb], dma="c_gwb")
    P.op("pool", lambda h: h.dma_start(out=s_gbf.ap(), in_=d_gbh), writes=[bgbf], dma="c_gbf")

    P.op("act", lambda h: h.activation(s_b2s.ap(), s_b2f.ap(), AF.Copy, scale=C_B2),
         reads=[bb2f], writes=[bb2s])
    P.op("act", lambda h: h.activation(s_sb2h.ap(), s_sb2h.ap(), AF.Copy, scale=0.5),
         reads=[bsb2h], writes=[bsb2h])
    P.op("act", lambda h: h.activation(s_ab32.ap(), s_xisfm.ap(), AF.Exp),
         reads=[bxisfm], writes=[bab32])
    P.op("dve", lambda h: h.tensor_scalar(s_ab32.ap(), s_ab32.ap(), 1.0 / 32.0, None, ALU.mult),
         reads=[bab32], writes=[bab32])
    P.op("dve", lambda h: h.memset(s_eps.ap(), EPS), writes=[beps])
    P.op("pe", lambda h: h.matmul(pb[7].ap()[:, 0:8], s_onesb.ap()[0:1, :], s_gbf.ap(),
                                  start=True, stop=True),
         reads=[bonesb, bgbf], writes=[bank[7]])
    P.op("dve", lambda h: h.tensor_copy(s_gbb.ap(), pb[7].ap()[:, 0:8]),
         reads=[bank[7]], writes=[bgbb])

    pump_q = []
    pumped = [0]
    ring_pos = {"w1": 0, "w2": 0}
    s1_count = [0]
    w2_count = [0]

    def queue_expert(e):
        for c in range(64):
            pump_q.append(("w1", d_ws1[e, c]))
        for j in range(8):
            pump_q.append(("w2", d_ws2[e, j][:, 0:1024]))
            pump_q.append(("w2", d_ws2[e, j][:, 1024:2048]))

    CAP = {"w1": 4, "w2": 3}

    def pump(n=1):
        for _ in range(n):
            if pumped[0] >= len(pump_q):
                return
            kind, src = pump_q[pumped[0]]
            consumed = s1_count[0] if kind == "w1" else w2_count[0]
            if ring_pos[kind] - consumed >= CAP[kind]:
                return
            pumped[0] += 1
            if kind == "w1":
                slot = ring_pos["w1"] % 4
                ring_pos["w1"] += 1
                P.op("pool", lambda h, src=src, slot=slot: h.dma_start(
                    out=s_wr[slot].ap(), in_=src), writes=[bwr[slot]], dma=f"wr{slot}")
            else:
                slot = ring_pos["w2"] % 3
                ring_pos["w2"] += 1
                P.op("pool", lambda h, src=src, slot=slot: h.dma_start(
                    out=s_w2r[slot].ap(), in_=src), writes=[bw2r[slot]], dma=f"w2r{slot}")

    for g in range(16):
        sl = g % 4
        P.op("sp", lambda h, g=g, sl=sl: h.dma_start(
            out=s_io[sl].ap(), in_=d_xin[g * 128:(g + 1) * 128, :]),
            writes=[bio[sl]], dma=f"io{sl}")
        for half in range(2):
            bk_i = (g % 2) * 2 + half

            def tfn(h, sl=sl, half=half, bk_i=bk_i):
                ins = None
                for q in range(4):
                    dc = half * 4 + q
                    ins = h.transpose(pb[bk_i].ap()[:, q * 128:(q + 1) * 128],
                                      s_io[sl].ap()[:, dc * 128:(dc + 1) * 128],
                                      s_ident.ap())
                return ins
            P.op("pe", tfn, reads=[bio[sl], bident], writes=[bank[bk_i]])
            P.op("dve", lambda h, g=g, half=half, bk_i=bk_i: h.tensor_copy(
                s_x.ap()[:, half * 4:(half + 1) * 4, g * 128:(g + 1) * 128],
                pb[bk_i].ap().rearrange("p (a b) -> p a b", b=128)),
                reads=[bank[bk_i]], writes=[bx[half * 4 + q] for q in range(4)])

    if identity_only:
        _emit_out = True
    else:
        _emit_out = True

    def emit_ln(want_nsq=False):
        for dc in range(8):
            q = dc % 2
            P.op("act", lambda h, dc=dc, q=q: h.activation(
                s_xb2[q].ap(), s_x.ap()[:, dc, :], AF.Copy),
                reads=[bx[dc]], writes=[bxb2[q]])
            P.op("dve", lambda h, q=q: h.tensor_tensor(
                s_xsq2[q].ap(), s_xb2[q].ap(), s_xb2[q].ap(), ALU.mult),
                reads=[bxb2[q]], writes=[bxsq2[q]])

            def mmfn(h, dc=dc, q=q):
                ins = None
                for s in range(NS):
                    ins = h.matmul(pb[s].ap(), s_onesb.ap(),
                                   s_xb2[q].ap()[:, s * SL:(s + 1) * SL],
                                   start=(dc == 0), stop=(dc == 7))
                for s in range(NS):
                    ins = h.matmul(pb[4 + s].ap(), s_onesb.ap(),
                                   s_xsq2[q].ap()[:, s * SL:(s + 1) * SL],
                                   start=(dc == 0), stop=(dc == 7))
                return ins
            P.op("pe", mmfn, reads=[bxb2[q], bxsq2[q], bonesb], writes=bank[0:8])
        for s in range(NS):
            pp = s % 2
            P.op("act", lambda h, s=s, pp=pp: h.activation(
                s_mb[pp].ap(), pb[s].ap(), AF.Copy, scale=1.0 / 1024.0),
                reads=[bank[s]], writes=[bmb[pp]])
            P.op("act", lambda h, s=s, pp=pp: h.activation(
                s_rstd[pp].ap(), pb[4 + s].ap(), AF.Copy, scale=1.0 / 1024.0),
                reads=[bank[4 + s]], writes=[brstd[pp]])
            P.op("act", lambda h, pp=pp: h.activation(s_cA[pp].ap(), s_mb[pp].ap(), AF.Square),
                 reads=[bmb[pp]], writes=[bcA[pp]])
            P.op("dve", lambda h, pp=pp: h.tensor_tensor(
                s_rstd[pp].ap(), s_rstd[pp].ap(), s_cA[pp].ap(), ALU.subtract),
                reads=[brstd[pp], bcA[pp]], writes=[brstd[pp]])
            P.op("act", lambda h, pp=pp: h.activation(s_cA[pp].ap(), s_rstd[pp].ap(),
                                                      AF.Ln, bias=s_eps.ap()),
                 reads=[brstd[pp], beps], writes=[bcA[pp]])
            if want_nsq:
                P.op("act", lambda h, pp=pp: h.activation(s_cB[pp].ap(), s_cA[pp].ap(),
                                                          AF.Exp, scale=-1.0),
                     reads=[bcA[pp]], writes=[bcB[pp]])
                P.op("dve", lambda h, s=s, pp=pp: h.scalar_tensor_tensor(
                    nsq_row(s),
                    s_rstd[pp].ap()[0:1, :], 1024.0, s_cB[pp].ap()[0:1, :],
                    ALU.mult, ALU.mult),
                    reads=[brstd[pp], bcB[pp]], writes=[bnsq2[s]])
            P.op("act", lambda h, pp=pp: h.activation(s_rstd[pp].ap(), s_cA[pp].ap(),
                                                      AF.Exp, scale=-0.5),
                 reads=[bcA[pp]], writes=[brstd[pp]])
            for dc in range(8):
                eng = "dve"
                P.op(eng, lambda h, s=s, pp=pp, dc=dc: h.tensor_tensor(
                    s_cB[pp].ap(), s_x.ap()[:, dc, s * SL:(s + 1) * SL],
                    s_mb[pp].ap(), ALU.subtract),
                    reads=[bx[dc], bmb[pp]], writes=[bcB[pp]])
                P.op(eng, lambda h, s=s, pp=pp, dc=dc: h.tensor_tensor(
                    s_xln.ap()[:, dc, s * SL:(s + 1) * SL], s_cB[pp].ap(),
                    s_rstd[pp].ap(), ALU.mult),
                    reads=[bcB[pp], brstd[pp]], writes=[bxln2[dc][s // 2]])

    def r3(ap_, b=8):
        return ap_.rearrange("p (a b) -> p a b", b=b)

    def glog_s(s):
            def gmm(h, s=s):
                ins = None
                for dc in range(8):
                    ins = h.matmul(pb[s].ap()[0:8, :],
                                   s_gwb.ap()[:, dc * 8:(dc + 1) * 8],
                                   s_xln.ap()[:, dc, s * SL:(s + 1) * SL],
                                   start=(dc == 0), stop=(dc == 7))
                return ins
            P.op("pe", gmm, reads=[bgwb] + [bxln2[dc][s // 2] for dc in range(8)],
                 writes=[bank[s]])
            P.op("act", lambda h, s=s: h.activation(
                s_glog.ap()[:, s * SL:(s + 1) * SL], pb[s].ap()[0:8, :], AF.Copy),
                reads=[bank[s]], writes=[bglog])

    def emit_gate():
        for s in (0, 1):
            glog_s(s)

    def emit_gate_rest():
        for s in (2, 3):
            glog_s(s)
        for g in range(16):
            bk_i = 4 + g % 2
            P.op("pe", lambda h, g=g, bk_i=bk_i: h.transpose(
                pb[bk_i].ap()[:, 0:8], s_glog.ap()[0:8, g * 128:(g + 1) * 128],
                s_ident.ap()[0:8, 0:8]),
                reads=[bglog, bident], writes=[bank[bk_i]])
            P.op("dve", lambda h, g=g, bk_i=bk_i: h.tensor_copy(
                s_gtm.ap()[:, g * 8:(g + 1) * 8], pb[bk_i].ap()[:, 0:8]),
                reads=[bank[bk_i]], writes=[bgtm])
        P.op("dve", lambda h: h.reduce_max(r3(s_gw1.ap(), 1), r3(s_gtm.ap()), axis=AX.X),
             reads=[bgtm], writes=[bgw1])
        P.op("dve", lambda h: h.tensor_tensor(
            r3(s_gsc.ap()), r3(s_gtm.ap()),
            r3(s_gw1.ap(), 1).broadcast_to((128, 16, 8)), ALU.subtract),
            reads=[bgtm, bgw1], writes=[bgsc])
        P.op("act", lambda h: h.activation(s_gsc.ap(), s_gsc.ap(), AF.Exp),
             reads=[bgsc], writes=[bgsc])
        P.op("dve", lambda h: h.reduce_sum(r3(s_gw1.ap(), 1), r3(s_gsc.ap()), axis=AX.X),
             reads=[bgsc], writes=[bgw1])
        P.op("dve", lambda h: h.reciprocal(s_gw1.ap(), s_gw1.ap()),
             reads=[bgw1], writes=[bgw1])
        P.op("dve", lambda h: h.tensor_tensor(
            r3(s_gsc.ap()), r3(s_gsc.ap()),
            r3(s_gw1.ap(), 1).broadcast_to((128, 16, 8)), ALU.mult),
            reads=[bgsc, bgw1], writes=[bgsc])
        P.op("dve", lambda h: h.tensor_tensor(
            r3(s_gm.ap()), r3(s_gsc.ap()),
            r3(s_gbb.ap()).broadcast_to((128, 16, 8)), ALU.add),
            reads=[bgsc, bgbb], writes=[bgm])
        P.op("dve", lambda h: h.reduce_max(r3(s_gw1.ap(), 1), r3(s_gm.ap()), axis=AX.X),
             reads=[bgm], writes=[bgw1])
        P.op("dve", lambda h: h.tensor_tensor(
            r3(s_geq.ap()), r3(s_gm.ap()),
            r3(s_gw1.ap(), 1).broadcast_to((128, 16, 8)), ALU.is_equal),
            reads=[bgm, bgw1], writes=[bgeq])
        P.op("dve", lambda h: h.scalar_tensor_tensor(
            s_gm.ap(), s_geq.ap(), -1e30, s_gm.ap(), ALU.mult, ALU.add),
            reads=[bgeq, bgm], writes=[bgm])
        P.op("dve", lambda h: h.reduce_max(r3(s_gw2.ap(), 1), r3(s_gm.ap()), axis=AX.X),
             reads=[bgm], writes=[bgw2])
        P.op("dve", lambda h: h.tensor_tensor(
            r3(s_gm.ap()), r3(s_gm.ap()),
            r3(s_gw2.ap(), 1).broadcast_to((128, 16, 8)), ALU.is_equal),
            reads=[bgm, bgw2], writes=[bgm])
        P.op("dve", lambda h: h.tensor_tensor(s_geq.ap(), s_geq.ap(), s_gm.ap(), ALU.add),
             reads=[bgeq, bgm], writes=[bgeq])
        P.op("dve", lambda h: h.scalar_tensor_tensor(
            s_ctm.ap(), s_geq.ap(), C_EFF, s_gsc.ap(), ALU.mult, ALU.mult),
            reads=[bgeq, bgsc], writes=[bctm])

    def emit_combfm():
        for g in range(16):
            bk_i = 4 + g // 4
            qq = g % 4
            P.op("pe", lambda h, g=g, bk_i=bk_i, qq=qq: h.transpose(
                pb[bk_i].ap()[0:8, qq * 128:(qq + 1) * 128],
                s_ctm.ap()[:, g * 8:(g + 1) * 8], s_ident.ap()),
                reads=[bctm, bident], writes=[bank[bk_i]])
        for q in range(4):
            P.op("act", lambda h, q=q: h.activation(
                s_combfm.ap()[:, q * 512:(q + 1) * 512], pb[4 + q].ap()[0:8, :], AF.Copy),
                reads=[bank[4 + q]], writes=[bcombfm])

    def emit_stage1(e, post_i0=None):
        for i in range(NI):
            if i == 1 and post_i0 is not None:
                post_i0()
            c0 = s1_count[0]
            slots = [(c0 + k) % 4 for k in range(4)]
            if ldw4:
                units = [
                    ("g3", "ab", slots[0], slots[1]),
                    ("l3", "ab", slots[2], slots[3]),
                ]
            else:
                units = [
                    ("g3", "a", slots[0], slots[1]),
                    ("g3", "b", slots[0], slots[1]),
                    ("l3", "a", slots[2], slots[3]),
                    ("l3", "b", slots[2], slots[3]),
                ]
            for (mat, half, w1slot, w3slot) in units:
                if half == "ab":
                    def mmfn(h, w1slot=w1slot, w3slot=w3slot):
                        ins = None
                        for dc in range(8):
                            lhsT = s_wr[w1slot].ap()[:, dc * 128:(dc + 1) * 128]
                            for s in range(NS):
                                ins = h.matmul(pb[s].ap(), lhsT,
                                               s_xln.ap()[:, dc, s * SL:(s + 1) * SL],
                                               start=(dc == 0), stop=(dc == 7))
                        for dc in range(8):
                            lhsT = s_wr[w3slot].ap()[:, dc * 128:(dc + 1) * 128]
                            for s in range(NS):
                                ins = h.matmul(pb[4 + s].ap(), lhsT,
                                               s_xln.ap()[:, dc, s * SL:(s + 1) * SL],
                                               start=(dc == 0), stop=(dc == 7))
                        return ins
                    P.op("pe", mmfn, reads=[bwr[w1slot], bwr[w3slot]] + bxln,
                         writes=bank[0:8])
                    s1_count[0] += 2
                    pump(2)
                    halves = [("a", 0), ("b", 2)]
                else:
                    s0 = 0 if half == "a" else 2
                    hf = 0 if half == "a" else 1
                    base = 0 if half == "a" else 4
                    p0 = base + 2

                    def mmfn(h, w1slot=w1slot, w3slot=w3slot, s0=s0, base=base):
                        ins = None
                        for dc in range(8):
                            lhsT = s_wr[w1slot].ap()[:, dc * 128:(dc + 1) * 128]
                            for k in range(2):
                                ins = h.matmul(pb[base + k].ap(), lhsT,
                                               s_xln.ap()[:, dc, (s0 + k) * SL:(s0 + k + 1) * SL],
                                               start=(dc == 0), stop=(dc == 7))
                        for dc in range(8):
                            lhsT = s_wr[w3slot].ap()[:, dc * 128:(dc + 1) * 128]
                            for k in range(2):
                                ins = h.matmul(pb[base + 2 + k].ap(), lhsT,
                                               s_xln.ap()[:, dc, (s0 + k) * SL:(s0 + k + 1) * SL],
                                               start=(dc == 0), stop=(dc == 7))
                        return ins
                    P.op("pe", mmfn, reads=[bwr[w1slot], bwr[w3slot]] +
                         [bxln2[dc][hf] for dc in range(8)],
                         writes=[bank[base + q] for q in range(4)])
                    if half == "b":
                        s1_count[0] += 2
                    pump(2)
                    halves = None
                if mat == "g3":
                    col = e * 32 + i
                    if half == "ab":
                        kpairs = [(s, 4 + s, s) for s in range(4)]
                    else:
                        kpairs = [(p0 - 2 + k, p0 + k, s0 + k) for k in range(2)]
                    for (b1k, b3k, s) in kpairs:
                        pp = s % 2
                        P.op("dve", lambda h, pp=pp, b3k=b3k, col=col: h.tensor_scalar(
                            s_cA[pp].ap(), pb[b3k].ap(),
                            s_b3.ap()[:, col:col + 1], None, ALU.add),
                            reads=[bank[b3k], bb3], writes=[bcA[pp]])
                        P.op("dve", lambda h, pp=pp, b1k=b1k, col=col: h.scalar_tensor_tensor(
                            s_cB[pp].ap(), pb[b1k].ap(), s_b1.ap()[:, col:col + 1],
                            s_cA[pp].ap(), ALU.add, ALU.mult),
                            reads=[bank[b1k], bb1, bcA[pp]], writes=[bcB[pp]])
                        P.op("dve", lambda h, pp=pp: h.tensor_scalar(
                            s_cA[pp].ap(), s_cB[pp].ap(), LIMIT, None, ALU.min),
                            reads=[bcB[pp]], writes=[bcA[pp]])
                        gslot = s
                        P.op("act", lambda h, pp=pp, gslot=gslot: h.activation(
                            s_gs[gslot].ap(), s_cA[pp].ap(), AF.Silu, scale=ALPHA),
                            reads=[bcA[pp]], writes=[bgs[gslot]])
                if mat == "l3":
                    coll = e * 32 + i + 16
                    if half == "ab":
                        kpairs = [(s, 4 + s, s) for s in range(4)]
                    else:
                        kpairs = [(p0 - 2 + k, p0 + k, s0 + k) for k in range(2)]
                    for (b1k, b3k, s) in kpairs:
                        pp = s % 2
                        gslot = s
                        P.op("dve", lambda h, pp=pp, b3k=b3k, coll=coll: h.tensor_scalar(
                            s_cA[pp].ap(), pb[b3k].ap(),
                            s_b3.ap()[:, coll:coll + 1], None, ALU.add),
                            reads=[bank[b3k], bb3], writes=[bcA[pp]])
                        P.op("dve", lambda h, pp=pp, b1k=b1k, coll=coll: h.scalar_tensor_tensor(
                            s_cB[pp].ap(), pb[b1k].ap(), s_b1.ap()[:, coll:coll + 1],
                            s_cA[pp].ap(), ALU.add, ALU.mult),
                            reads=[bank[b1k], bb1, bcA[pp]], writes=[bcB[pp]])
                        P.op("dve", lambda h, pp=pp: h.tensor_scalar(
                            s_cA[pp].ap(), s_cB[pp].ap(), -LIMIT, LIMIT, ALU.max, ALU.min),
                            reads=[bcB[pp]], writes=[bcA[pp]])
                        P.op("dve", lambda h, s=s, pp=pp, i=i, gslot=gslot: h.scalar_tensor_tensor(
                            s_h.ap()[:, i, s * SL:(s + 1) * SL], s_cA[pp].ap(), 1.0,
                            s_gs[gslot].ap(), ALU.add, ALU.mult),
                            reads=[bcA[pp], bgs[gslot]], writes=[bh[i]])

    def emit_stage2(e, shared):
        if not shared:
            def cbm(h, e=e):
                ins = None
                for s in range(NS):
                    ins = h.matmul(pb[s].ap(), s_selb.ap()[:, e * 128:(e + 1) * 128],
                                   s_combfm.ap()[:, s * SL:(s + 1) * SL],
                                   start=True, stop=True)
                return ins
            P.op("pe", cbm, reads=[bselb, bcombfm], writes=bank[0:4])
            for s in range(NS):
                P.op("act", lambda h, s=s: h.activation(
                    s_combbc.ap()[:, s * SL:(s + 1) * SL], pb[s].ap(), AF.Copy),
                    reads=[bank[s]], writes=[bcombbc])
        for j in range(8):
            slotA = w2_count[0] % 3
            slotB = (w2_count[0] + 1) % 3
            for half in range(2):
                p0 = (2 * j + half) % 4 * 2
                s0 = half * 2

                def mmfn_lo(h, slotA=slotA, p0=p0, s0=s0, shared=shared, j=j):
                    ins = None
                    if shared:
                        for k in range(2):
                            ins = h.matmul(pb[p0 + k].ap(),
                                           s_b2s.ap()[:, j * 128:(j + 1) * 128],
                                           s_combfm.ap()[:, (s0 + k) * SL:(s0 + k + 1) * SL],
                                           start=True, stop=False)
                    for ic in range(8):
                        lhsT = s_w2r[slotA].ap()[:, ic * 128:(ic + 1) * 128]
                        for k in range(2):
                            ins = h.matmul(pb[p0 + k].ap(), lhsT,
                                           s_h.ap()[:, ic, (s0 + k) * SL:(s0 + k + 1) * SL],
                                           start=(ic == 0 and not shared), stop=False)
                    return ins

                def mmfn_hi(h, slotB=slotB, p0=p0, s0=s0):
                    ins = None
                    for ic in range(8, 16):
                        lhsT = s_w2r[slotB].ap()[:, (ic % 8) * 128:(ic % 8 + 1) * 128]
                        for k in range(2):
                            ins = h.matmul(pb[p0 + k].ap(), lhsT,
                                           s_h.ap()[:, ic, (s0 + k) * SL:(s0 + k + 1) * SL],
                                           start=False, stop=(ic == 15))
                    return ins
                rd = [bw2r[slotA]] + bh[0:8]
                if shared:
                    rd += [bb2s, bcombfm]
                P.op("pe", mmfn_lo, reads=rd, writes=[bank[p0], bank[p0 + 1]])
                P.op("pe", mmfn_hi, reads=[bw2r[slotB]] + bh[8:16],
                     writes=[bank[p0], bank[p0 + 1]])
                if half == 1:
                    w2_count[0] += 2
                pump(1)
                for k in range(2):
                    s = s0 + k
                    pp = s % 2
                    if shared:
                        P.op("dve", lambda h, s=s, p0=p0, k=k, j=j: h.scalar_tensor_tensor(
                            s_x.ap()[:, j, s * SL:(s + 1) * SL], pb[p0 + k].ap(), C_SH,
                            s_x.ap()[:, j, s * SL:(s + 1) * SL], ALU.mult, ALU.add),
                            reads=[bank[p0 + k], bx[j]], writes=[bx[j]])
                    else:
                        P.op("dve", lambda h, s=s, pp=pp, p0=p0, k=k: h.tensor_tensor(
                            s_cA[pp].ap(), pb[p0 + k].ap(),
                            s_combbc.ap()[:, s * SL:(s + 1) * SL], ALU.mult),
                            reads=[bank[p0 + k], bcombbc], writes=[bcA[pp]])
                        P.op("dve", lambda h, s=s, pp=pp, j=j: h.tensor_tensor(
                            s_x.ap()[:, j, s * SL:(s + 1) * SL],
                            s_x.ap()[:, j, s * SL:(s + 1) * SL],
                            s_cA[pp].ap(), ALU.add),
                            reads=[bcA[pp], bx[j]], writes=[bx[j]])
                if shared and half == 1 and j < 8:
                    P.op("dve", lambda h, j=j: h.tensor_scalar(
                        s_x.ap()[:, j, :], s_x.ap()[:, j, :],
                        s_sb2h.ap()[:, j:j + 1], None, ALU.add),
                        reads=[bx[j], bsb2h], writes=[bx[j]])
    def emit_ffn():
        emit_gate()
        for e in range(n_experts):
            pump(4)
            def _hook():
                emit_gate_rest()
                emit_combfm()
            emit_stage1(e, post_i0=(_hook if e == 0 else None))
            if dbg == "h" and e == 0:
                for ic in range(16):
                    for hf in range(2):
                        P.op("act", lambda h, ic=ic, hf=hf: h.activation(
                            s_io[0].ap(), s_h.ap()[:, ic, hf * 1024:(hf + 1) * 1024], AF.Copy),
                            reads=[bh[ic]], writes=[bio[0]])
                        P.op("sp", lambda h, ic=ic, hf=hf: h.dma_start(
                            out=d_dbgh[:, ic * TPC + hf * 1024: ic * TPC + (hf + 1) * 1024],
                            in_=s_io[0].ap()), reads=[bio[0]], dma="dbg")
                for dc in range(8):
                    for hf in range(2):
                        P.op("act", lambda h, dc=dc, hf=hf: h.activation(
                            s_io[0].ap(), s_xln.ap()[:, dc, hf * 1024:(hf + 1) * 1024], AF.Copy),
                            reads=[bxln[dc]], writes=[bio[0]])
                        P.op("sp", lambda h, dc=dc, hf=hf: h.dma_start(
                            out=d_dbgx[:, dc * TPC + hf * 1024: dc * TPC + (hf + 1) * 1024],
                            in_=s_io[0].ap()), reads=[bio[0]], dma="dbg")
            emit_stage2(e, shared=(e == 8))

    def emit_attn_prep():
        P.op("pool", lambda h: h.dma_start(
            out=s_pjb.ap().rearrange("p a b -> p (a b)"),
            in_=d_pjh), writes=[bpjb], dma="c_pjb")
        P.op("sp", lambda h: h.dma_start(out=s_msab.ap(), in_=d_xisrow),
             writes=[bmsab], dma="c_msab")
        P.op("dve", lambda h: h.tensor_scalar(s_msab.ap(), s_msab.ap(), -0.5, None, ALU.mult),
             reads=[bmsab], writes=[bmsab])
        iwc = [0]

        def iw_chunk(dc):
            slot = iwc[0] % 4
            iwc[0] += 1
            P.op("pool", lambda h, dc=dc, slot=slot: h.dma_start(
                out=s_wr[slot].ap(), in_=d_iwh[:, dc * 1024:(dc + 1) * 1024]),
                writes=[bwr[slot]], dma=f"wr{slot}")
            return slot

        for dc in range(8):
            slot = iw_chunk(dc)
            q = dc % 2
            P.op("act", lambda h, slot=slot, q=q: h.activation(
                s_sqA2[q].ap(), s_wr[slot].ap(), AF.Square),
                reads=[bwr[slot]], writes=[bsqA2[q]])
            P.op("pe", lambda h, dc=dc, q=q: h.matmul(
                pb[6].ap()[0:1, :], s_onesb.ap()[:, 0:1],
                s_sqA2[q].ap()[:, 0:512],
                start=(dc == 0), stop=(dc == 7)),
                reads=[bsqA2[q], bonesb], writes=[bank[6]])
            P.op("pe", lambda h, dc=dc, q=q: h.matmul(
                pb[7].ap()[0:1, :], s_onesb.ap()[:, 0:1],
                s_sqA2[q].ap()[:, 512:1024],
                start=(dc == 0), stop=(dc == 7)),
                reads=[bsqA2[q], bonesb], writes=[bank[7]])
        P.op("dve", lambda h: h.tensor_copy(s_niw.ap()[:, 0:512], pb[6].ap()[0:1, :]),
             reads=[bank[6]], writes=[bniw])
        P.op("dve", lambda h: h.tensor_copy(s_niw.ap()[:, 512:1024], pb[7].ap()[0:1, :]),
             reads=[bank[7]], writes=[bniw])
        for os in range(2):
            for dc in range(8):
                slot = iw_chunk(dc)

                def wrmm(h, dc=dc, os=os, slot=slot):
                    ins = None
                    for rc in range(8):
                        ins = h.matmul(pb[rc].ap(),
                                       s_pjb.ap()[:, rc, dc * 128:(dc + 1) * 128],
                                       s_wr[slot].ap()[:, os * 512:(os + 1) * 512],
                                       start=(dc == 0), stop=False)
                    return ins
                P.op("pe", wrmm, reads=[bpjb, bwr[slot]],
                     writes=bank[0:8])

            def wcor(h, os=os):
                ins = None
                for rc in range(8):
                    ins = h.matmul(pb[rc].ap(),
                                   s_msab.ap()[:, rc * 128:(rc + 1) * 128],
                                   s_niw.ap()[:, os * 512:(os + 1) * 512],
                                   start=False, stop=True)
                return ins
            P.op("pe", wcor, reads=[bmsab, bniw], writes=bank[0:8])
            for rc in range(8):
                pp = rc % 2
                P.op("act", lambda h, rc=rc, os=os, pp=pp: h.activation(
                    s_att[pp].ap(), pb[rc].ap(), AF.Exp,
                    scale=s_xisfm.ap()[:, rc:rc + 1]),
                    reads=[bank[rc], bxisfm], writes=[batt[pp]])
                P.op("dve", lambda h, rc=rc, os=os, pp=pp: h.tensor_scalar(
                    s_wrm.ap()[:, rc * 1024 + os * 512: rc * 1024 + (os + 1) * 512],
                    s_att[pp].ap(), s_ab32.ap()[:, rc:rc + 1], None, ALU.mult),
                    reads=[batt[pp], bab32], writes=[bwrm])
    def emit_attn_main():
        for rc in range(8):
            for hf in range(2):
                g0 = ((2 * rc + hf) % 4) * 2
                s0 = hf * 2

                def ftm(h, rc=rc, g0=g0, s0=s0):
                    ins = None
                    for dc in range(8):
                        lhsT = s_pjb.ap()[:, rc, dc * 128:(dc + 1) * 128]
                        for k in range(2):
                            ins = h.matmul(pb[g0 + k].ap(), lhsT,
                                           s_xln.ap()[:, dc, (s0 + k) * SL:(s0 + k + 1) * SL],
                                           start=(dc == 0), stop=False)
                    for k in range(2):
                        ins = h.matmul(pb[g0 + k].ap(),
                                       s_msab.ap()[:, rc * 128:(rc + 1) * 128],
                                       nsq_row(s0 + k),
                                       start=False, stop=True)
                    return ins
                P.op("pe", ftm, reads=[bpjb, bmsab, bnsq2[s0], bnsq2[s0 + 1]] +
                     [bxln2[dc][hf] for dc in range(8)],
                     writes=[bank[g0], bank[g0 + 1]])
                for k in range(2):
                    s = s0 + k
                    pp = s % 2
                    P.op("act", lambda h, rc=rc, g0=g0, k=k, pp=pp: h.activation(
                        s_att[pp].ap(), pb[g0 + k].ap(), AF.Exp,
                        scale=s_xisfm.ap()[:, rc:rc + 1]),
                        reads=[bank[g0 + k], bxisfm], writes=[batt[pp]])
                    P.op("dve", lambda h, rc=rc, s=s, pp=pp, hf=hf: h.tensor_scalar(
                        s_feats.ap()[:, rc, s * SL:(s + 1) * SL],
                        s_att[pp].ap(), s_ab32.ap()[:, rc:rc + 1], None, ALU.mult),
                        reads=[batt[pp], bab32], writes=[bfeats2[rc][hf]])
        for oc in range(8):
            for hf in range(2):
                g0 = ((2 * oc + hf) % 4) * 2
                s0 = hf * 2

                def aom(h, oc=oc, g0=g0, s0=s0):
                    ins = None
                    for rc in range(8):
                        lhsT = s_wrm.ap()[:, rc * 1024 + oc * 128: rc * 1024 + (oc + 1) * 128]
                        for k in range(2):
                            ins = h.matmul(pb[g0 + k].ap(), lhsT,
                                           s_feats.ap()[:, rc, (s0 + k) * SL:(s0 + k + 1) * SL],
                                           start=(rc == 0), stop=(rc == 7))
                    return ins
                P.op("pe", aom, reads=[bwrm] + [bfeats2[rc][hf] for rc in range(8)],
                     writes=[bank[g0], bank[g0 + 1]])
                for k in range(2):
                    s = s0 + k
                    P.op("dve", lambda h, oc=oc, g0=g0, s=s, k=k: h.tensor_tensor(
                        s_x.ap()[:, oc, s * SL:(s + 1) * SL],
                        s_x.ap()[:, oc, s * SL:(s + 1) * SL],
                        pb[g0 + k].ap(), ALU.add),
                        reads=[bank[g0 + k], bx[oc]], writes=[bx[oc]])

    if not identity_only:
        for e in range(n_experts):
            queue_expert(e)
        pump(4)
        emit_ln()
        emit_ffn()
        if do_attn:
            emit_attn_prep()
            emit_ln(want_nsq=True)
            if dbg == "nsq":
                P.op("dve", lambda h: h.tensor_copy(s_io[0].ap()[0:1, 0:1024], s_nsqA.ap()),
                     reads=[bnsq2[0], bnsq2[1]], writes=[bio[0]])
                P.op("dve", lambda h: h.tensor_copy(s_io[1].ap()[0:1, 0:1024], s_nsqB.ap()),
                     reads=[bnsq2[2], bnsq2[3]], writes=[bio[1]])
                P.op("sp", lambda h: h.dma_start(out=d_dbgx[0:1, 0:1024], in_=s_io[0].ap()[0:1, 0:1024]),
                     reads=[bio[0]], dma="dbg")
                P.op("sp", lambda h: h.dma_start(out=d_dbgx[0:1, 1024:2048], in_=s_io[1].ap()[0:1, 0:1024]),
                     reads=[bio[1]], dma="dbg")
            emit_attn_main()
        if do_pass2:
            for e in range(n_experts):
                queue_expert(e)
            emit_ln()
            emit_ffn()

    for g in range(16):
        sl = g % 4
        for half in range(2):
            bk_i = (g % 2) * 2 + half

            def tofn(h, g=g, half=half, bk_i=bk_i):
                ins = None
                for q in range(4):
                    dc = half * 4 + q
                    ins = h.transpose(pb[bk_i].ap()[:, q * 128:(q + 1) * 128],
                                      s_x.ap()[:, dc, g * 128:(g + 1) * 128],
                                      s_ident.ap())
                return ins
            P.op("pe", tofn, reads=[bx[half * 4 + q] for q in range(4)] + [bident],
                 writes=[bank[bk_i]])
            if half == 0:
                P.op("dve", lambda h, half=half, bk_i=bk_i, sl=sl: h.tensor_copy(
                    s_io[sl].ap()[:, half * 512:(half + 1) * 512], pb[bk_i].ap()),
                    reads=[bank[bk_i]], writes=[bio[sl]])
            else:
                P.op("act", lambda h, half=half, bk_i=bk_i, sl=sl: h.activation(
                    s_io[sl].ap()[:, half * 512:(half + 1) * 512], pb[bk_i].ap(), AF.Copy),
                    reads=[bank[bk_i]], writes=[bio[sl]])
        P.op("sp", lambda h, g=g, sl=sl: h.dma_start(
            out=d_out[g * 128:(g + 1) * 128, :], in_=s_io[sl].ap()),
            reads=[bio[sl]], writes=[P.buf(f"st{g}")], dma=f"st{sl}")

    for sl in range(4):
        nm = "d_st" + str(sl)
        if nm in P.cnt:
            final = P.cnt[nm]
            P.raw("sp", lambda h, nm=nm, final=final: h.wait_ge(P.sem[nm], final))

    with nc.Block() as blk:
        P.replay(blk)
    ctx.close()
    assert nc.sbuf_base <= BASE, f"bass consts collided with manual map: {nc.sbuf_base} > {BASE}"
    return nc


def kernel(**inputs):
    common, shards = prep_host(inputs)
    nc = build()
    in_maps = []
    for c in range(8):
        m = dict(common)
        m["xin"] = shards[c]
        in_maps.append(m)
    res = run_bass_kernel_spmd(nc, in_maps, core_ids=list(range(8)))
    out = np.concatenate([np.asarray(r["out"]) for r in res.results], axis=0)
    return out.reshape(4, 4096, 1024).astype(np.float32)
